# version 37
# baseline (speedup 1.0000x reference)
import contextlib

import numpy as np
import ml_dtypes
import concourse.bass as bass
import concourse.mybir as mybir
from concourse.bass_utils import run_bass_kernel_spmd

F32 = mybir.dt.float32
BF16 = mybir.dt.bfloat16
I32 = mybir.dt.int32
AF = mybir.ActivationFunctionType
ALU = mybir.AluOpType
AX = mybir.AxisListType

NB = 2
T = 2048
D = 1024
H = 3072
NTB = T // 128
NDT = D // 128
NHT = H // 128
TSB = 512
NTSB = T // TSB
EPS = 1e-5
INV127 = float(np.float32(1.0) / np.float32(127.0))
SCALE_G = INV127
SCALE_Y = INV127 * INV127

DEBUG_OUTPUTS = []


class Ctx:

    def __init__(self, live, proxies, sems, stash):
        self.live = live
        self.prox = proxies
        self.sems = sems
        self.cnt = {k: 0 for k in sems}
        self.stash = stash

    def wait(self, eng, sem, val):
        if val is None or val <= 0:
            return
        if self.live == eng:
            self.prox[eng].wait_ge(self.sems[sem], val)

    def do(self, eng, method, *a, inc=None, incval=1, **kw):
        if self.live == eng:
            ins = getattr(self.prox[eng], method)(*a, **kw)
            if inc is not None:
                ins.then_inc(self.sems[inc], incval)
        if inc is not None:
            self.cnt[inc] += incval

    def dma(self, eng, out, in_, sem, **kw):
        self.do(eng, "dma_start", inc=sem, incval=16, out=out, in_=in_, **kw)

    def put(self, key):
        if key not in self.stash:
            self.stash[key] = dict(self.cnt)

    def at(self, key, sem):
        return self.stash[key][sem]


class DStream:

    def __init__(self, c: Ctx, name, n):
        self.c, self.name, self.n = c, name, n
        self.guard = [None] * n
        self.items = {}
        self.i = 0

    @staticmethod
    def sem_names(name, n):
        return [f"{name}{i}" for i in range(n)]

    def load(self, eng, key, out_of_slot, in_ap, **kw):
        s = self.i % self.n
        self.i += 1
        g = self.guard[s]
        if g:
            self.c.wait(eng, g[0], g[1])
        sem = f"{self.name}{s}"
        self.c.dma(eng, out_of_slot(s), in_ap, sem, **kw)
        self.items[key] = (sem, self.c.cnt[sem])
        return s

    def consumed(self, slot, sem, val):
        self.guard[slot] = (sem, val)

    def wait_item(self, eng, key):
        sem, v = self.items[key]
        self.c.wait(eng, sem, v)

    def totals(self):
        out = {}
        for s in range(self.n):
            out[f"{self.name}{s}"] = self.c.cnt[f"{self.name}{s}"]
        return out


def build_program():
    nc = bass.Bass("TRN2", target_bir_lowering=False)

    x_hbm = nc.dram_tensor("x", [NB, T, D], F32, kind="ExternalInput").ap()
    wgT_hbm = nc.dram_tensor("wgT", [D, H], I32, kind="ExternalInput").ap()
    wuT_hbm = nc.dram_tensor("wuT", [D, H], I32, kind="ExternalInput").ap()
    wdT_hbm = nc.dram_tensor("wdT", [H, D], I32, kind="ExternalInput").ap()
    ident_hbm = nc.dram_tensor("ident", [128, 128], BF16, kind="ExternalInput").ap()
    out_hbm = nc.dram_tensor("out", [NB, T, D], F32, kind="ExternalOutput").ap()

    wg_scr = nc.dram_tensor("wg_scr", [D, H], BF16).ap()
    wu_scr = nc.dram_tensor("wu_scr", [D, H], BF16).ap()
    s_scr = nc.dram_tensor("s_scr", [NB, T, D], BF16).ap()

    dbg = {}
    for name in DEBUG_OUTPUTS:
        shp = {"xn": [NB, T, D], "xnT": [NB, D, T], "s": [NB, T, D]}[name]
        dt = {"xn": F32, "xnT": F32, "s": BF16}[name]
        dbg[name] = nc.dram_tensor("dbg_" + name, shp, dt, kind="ExternalOutput").ap()

    sem_names = (
        ["idld", "cvt", "cvtw", "bn", "rstd", "xnw",
         "Ttr", "Tev", "S", "mx", "E", "rec", "O", "sevc",
         "G", "U", "sG", "hact", "Y", "gev", "sgm", "oadd", "dbgs"]
        + DStream.sem_names("xld", 3)
        + DStream.sem_names("cvc", 2)
        + DStream.sem_names("cwc", 2)
        + DStream.sem_names("cvo", 2)
        + DStream.sem_names("spl", 2)
        + DStream.sem_names("srl", 2)
        + DStream.sem_names("xfl", 2)
        + DStream.sem_names("wgl", 2)
        + DStream.sem_names("wul", 2)
        + DStream.sem_names("ost", 2)
    )

    shared_stash = {}

    with contextlib.ExitStack() as stack:
        ent = stack.enter_context
        wdT_sb = ent(nc.sbuf_tensor("wdT_sb", [128, NHT * 1024], BF16))
        xn_sb = ent(nc.sbuf_tensor("xn_sb", [128, NTB * 1024], BF16))
        xnT_sb = ent(nc.sbuf_tensor("xnT_sb", [128, NDT * 2048], BF16))
        hact = ent(nc.sbuf_tensor("hact", [128, NHT * 512], BF16))
        Ebuf = ent(nc.sbuf_tensor("Ebuf", [128, 2 * 2048], BF16))
        aT = ent(nc.sbuf_tensor("aT", [128, 2048], BF16))
        wch = ent(nc.sbuf_tensor("wch", [128, 4 * 2048], BF16))
        xbuf = ent(nc.sbuf_tensor("xbuf", [128, 3 * 1024], F32))
        xfbuf = ent(nc.sbuf_tensor("xfbuf", [128, 2 * 1024], F32))
        outbuf = ent(nc.sbuf_tensor("outbuf", [128, 2 * 512], F32))
        sqb = ent(nc.sbuf_tensor("sqb", [128, 2 * 1024], BF16))
        srlb = ent(nc.sbuf_tensor("srlb", [128, 2 * 1024], BF16))
        gbuf = ent(nc.sbuf_tensor("gbuf", [128, 4 * 1024], BF16))
        sgbuf = ent(nc.sbuf_tensor("sgbuf", [128, 1024], BF16))
        sGb = ent(nc.sbuf_tensor("sGb", [128, 512], BF16))
        identt = ent(nc.sbuf_tensor("identt", [128, 128], BF16))
        mvAll = ent(nc.sbuf_tensor("mvAll", [128, 32], F32))
        bnst = ent(nc.sbuf_tensor("bnst", [128, 12], F32))
        veps = ent(nc.sbuf_tensor("veps", [128, 16], F32))
        lnt = ent(nc.sbuf_tensor("lnt", [128, 16], F32))
        rstd16 = ent(nc.sbuf_tensor("rstd16", [128, 16], F32))
        nmr16 = ent(nc.sbuf_tensor("nmr16", [128, 16], F32))
        nmx = ent(nc.sbuf_tensor("nmx", [128, 2], F32))
        sume = ent(nc.sbuf_tensor("sume", [128, 2], F32))
        recq = ent(nc.sbuf_tensor("recq", [128, 2], F32))
        psT = ent(nc.psum_tensor("psT", [128, 2048], BF16))
        psF = ent(nc.psum_tensor("psF", [128, 3072], F32))
        sems = {n: ent(nc.semaphore(n)) for n in sem_names}
        block = ent(nc.Block())

        def walk(live, proxies):
            c = Ctx(live, proxies, sems, shared_stash)
            mv2 = mvAll[:].rearrange("p (t two) -> p t two", two=2)
            mv_mean = mv2[:, :, 0]
            mv_var = mv2[:, :, 1]

            psS = psF[:, 0:2048]
            psG = psF[:, 0:1024]
            psU = psF[:, 1024:2048]
            psO = psF[:, 2048:3072]
            psY = psO

            xs = DStream(c, "xld", 3)
            spl = DStream(c, "spl", 2)
            srs = DStream(c, "srl", 2)
            xfs = DStream(c, "xfl", 2)
            wgs = DStream(c, "wgl", 2)
            wus = DStream(c, "wul", 2)
            obs = DStream(c, "ost", 2)
            Tstate = {"g": 0}
            Ehist = []

            c.dma("sync", identt[:], ident_hbm, "idld")

            stg_i32 = [hact[:, 0:6144].bitcast(I32), hact[:, 6144:12288].bitcast(I32)]
            cw_bf = [wch[:, 0:3072], wch[:, 3072:6144]]
            N_CONV = 16
            conv_state = {"loads": [], "done": 0, "out": 0}
            wd_state = {"done": 0, "loaded": 0, "lsem": []}

            def conversions_in():
                for k in range(16):
                    par = k % 2
                    if k >= 2:
                        c.wait("gpsimd", "cvt", k - 1)
                    src = (wgT_hbm, wuT_hbm)[k // 8]
                    ch = k % 8
                    c.dma("gpsimd", stg_i32[par],
                          src[ch * 128:(ch + 1) * 128, :], f"cvc{par}")
                    conv_state["loads"].append((f"cvc{par}", c.cnt[f"cvc{par}"]))

            def conv_cvt_step(n):
                for _ in range(n):
                    k = conv_state["done"]
                    if k >= N_CONV:
                        return
                    if 2 <= k < 16 and conv_state["out"] < k - 1:
                        return
                    conv_state["done"] += 1
                    par = k % 2
                    sem, v = conv_state["loads"][k]
                    c.wait("vector", sem, v)
                    if k >= 2:
                        c.wait("vector", f"cvo{par}", 16 * ((k - par) // 2))
                    c.do("vector", "tensor_copy", cw_bf[par], stg_i32[par],
                         inc="cvt")
                    if conv_state["done"] == N_CONV:
                        c.put("conv_all")

            stg_wd = [srlb[:, 0:2048].bitcast(I32), gbuf[:, 0:2048].bitcast(I32)]

            def conv_wd_step(n):
                for _ in range(n):
                    ht = wd_state["done"]
                    if ht >= NHT:
                        return
                    wd_state["done"] += 1
                    while wd_state["loaded"] <= min(ht + 1, NHT - 1):
                        lt = wd_state["loaded"]
                        wd_state["loaded"] += 1
                        sl = lt % 2
                        if lt >= 2:
                            c.wait("gpsimd", "cvtw", lt - 1)
                        c.dma("gpsimd", stg_wd[sl],
                              wdT_hbm[lt * 128:(lt + 1) * 128, :], f"cwc{sl}")
                        wd_state["lsem"].append((f"cwc{sl}", c.cnt[f"cwc{sl}"]))
                    sem, v = wd_state["lsem"][ht]
                    c.wait("gpsimd", sem, v)
                    c.do("gpsimd", "tensor_copy",
                         wdT_sb[:, ht * 1024:(ht + 1) * 1024],
                         stg_wd[ht % 2], inc="cvtw")

            def conv_out_step():
                while conv_state["out"] < min(conv_state["done"], 16):
                    k = conv_state["out"]
                    conv_state["out"] += 1
                    par = k % 2
                    c.wait("scalar", "cvt", k + 1)
                    dst = (wg_scr, wu_scr)[k // 8]
                    ch = k % 8
                    c.dma("scalar", dst[ch * 128:(ch + 1) * 128, :], cw_bf[par],
                          f"cvo{par}")
                    if conv_state["out"] == 16:
                        c.put("conv_done")

            def conv_wait(eng):
                c.wait(eng, "cvo0", c.at("conv_done", "cvo0"))
                c.wait(eng, "cvo1", c.at("conv_done", "cvo1"))

            def ln_phase(b, as_steps=False):
                steps = []

                def step(fn):
                    if as_steps:
                        steps.append(fn)
                    else:
                        fn()

                def x_eng(tb):
                    return "scalar" if (b == 0 and tb % 2 == 1) else "sync"

                def p1_tile(tb):
                    par = xs.load(x_eng(tb), ("p1", b, tb),
                                  lambda s: xbuf[:, s * 1024:(s + 1) * 1024],
                                  x_hbm[b, tb * 128:(tb + 1) * 128, :])
                    xs.wait_item("vector", ("p1", b, tb))
                    c.wait("vector", "bn", c.cnt["bn"])
                    c.do("vector", "bn_stats", bnst[:, 0:6],
                         xbuf[:, par * 1024: par * 1024 + 512], inc="bn")
                    c.wait("vector", "bn", c.cnt["bn"])
                    c.do("vector", "bn_stats", bnst[:, 6:12],
                         xbuf[:, par * 1024 + 512: par * 1024 + 1024], inc="bn")
                    c.wait("vector", "bn", c.cnt["bn"])
                    c.do("vector", "bn_aggr", mvAll[:, 2 * tb: 2 * tb + 2],
                         bnst[:, 0:12], inc="bn")
                    xs.consumed(par, "bn", c.cnt["bn"])

                nmr_holder = [None]

                def rstd_block():
                    c.wait("vector", "bn", c.cnt["bn"])
                    c.do("vector", "tensor_scalar_add", veps[:, 0:16], mv_var,
                         EPS, inc="bn")
                    c.wait("scalar", "bn", c.cnt["bn"])
                    c.do("scalar", "activation", lnt[:, 0:16], veps[:, 0:16],
                         AF.Ln, inc="rstd")
                    c.wait("scalar", "rstd", c.cnt["rstd"])
                    c.do("scalar", "activation", rstd16[:, 0:16], lnt[:, 0:16],
                         AF.Exp, scale=-0.5, inc="rstd")
                    c.wait("vector", "rstd", c.cnt["rstd"])
                    c.do("vector", "scalar_tensor_tensor", nmr16[:, 0:16],
                         mv_mean, -1.0, rstd16[:, 0:16], ALU.mult, ALU.mult,
                         inc="bn")
                    nmr_holder[0] = c.cnt["bn"]

                def p2_tile(tb):
                    eng = "gpsimd"
                    par = xs.load(x_eng(tb), ("p2", b, tb),
                                  lambda s: xbuf[:, s * 1024:(s + 1) * 1024],
                                  x_hbm[b, tb * 128:(tb + 1) * 128, :])
                    xs.wait_item(eng, ("p2", b, tb))
                    c.wait(eng, "bn", nmr_holder[0])
                    if b >= 1 and tb == 0:
                        c.wait(eng, "O", c.at(("attn", b - 1), "O"))
                    c.do(eng, "tensor_scalar",
                         xn_sb[:, tb * 1024:(tb + 1) * 1024],
                         xbuf[:, par * 1024:(par + 1) * 1024],
                         rstd16[:, tb:tb + 1], nmr16[:, tb:tb + 1],
                         ALU.mult, ALU.add, inc="xnw")
                    xs.consumed(par, "xnw", c.cnt["xnw"])
                    if b == 0:
                        xnT_tile(b, tb)
                        if tb % 4 == 3:
                            S0_chunk(b, tb // 4)
                    if tb == NTB - 1:
                        c.put(("xnw", b))

                def rstd_half(lo, hi):
                    c.wait("vector", "bn", c.cnt["bn"])
                    c.do("vector", "tensor_scalar_add", veps[:, lo:hi],
                         mv_var[:, lo:hi], EPS, inc="bn")
                    c.wait("scalar", "bn", c.cnt["bn"])
                    c.do("scalar", "activation", lnt[:, lo:hi], veps[:, lo:hi],
                         AF.Ln, inc="rstd")
                    c.wait("scalar", "rstd", c.cnt["rstd"])
                    c.do("scalar", "activation", rstd16[:, lo:hi], lnt[:, lo:hi],
                         AF.Exp, scale=-0.5, inc="rstd")
                    c.wait("vector", "rstd", c.cnt["rstd"])
                    c.do("vector", "scalar_tensor_tensor", nmr16[:, lo:hi],
                         mv_mean[:, lo:hi], -1.0, rstd16[:, lo:hi],
                         ALU.mult, ALU.mult, inc="bn")
                    nmr_holder[0] = c.cnt["bn"]

                if b == 0 and not as_steps:
                    for tb in range(4):
                        p1_tile(tb)
                    rstd_half(0, 4)
                    for q4 in range(1, 4):
                        for tb in range(4):
                            p1_tile(4 * q4 + tb)
                            p2_tile(4 * (q4 - 1) + tb)
                        rstd_half(4 * q4, 4 * q4 + 4)
                    for tb in range(12, NTB):
                        p2_tile(tb)
                    softmax_q(0, 0, embed_tr=False)
                    return None
                for tb in range(NTB):
                    step(lambda tb=tb: p1_tile(tb))
                step(rstd_block)
                for tb in range(NTB):
                    step(lambda tb=tb: p2_tile(tb))
                if as_steps:
                    return steps

            def xnT_tile(b, tb):
                if tb == 0:
                    c.put(("xnTbase", b))
                grp = Tstate["g"]
                bank = (grp % 2) * 1024
                c.wait("tensor", "Tev", grp - 1)
                c.wait("tensor", "xnw", b * NTB + tb + 1)
                if grp == 0:
                    c.wait("tensor", "idld", 16)
                for dt in range(NDT):
                    c.do("tensor", "transpose",
                         psT[:, bank + dt * 128: bank + (dt + 1) * 128],
                         xn_sb[:, tb * 1024 + dt * 128: tb * 1024 + (dt + 1) * 128],
                         identt[:], inc="Ttr" if dt == NDT - 1 else None)
                c.wait("vector", "Ttr", c.cnt["Ttr"])
                if b >= 1 and tb == 0:
                    c.wait("vector", "G", c.cnt["G"])
                    c.wait("vector", "U", c.cnt["U"])
                o = xnT_sb[:].bitcast(I32).rearrange(
                    "p (dt t) -> p dt t", dt=NDT)[:, :, tb * 64:(tb + 1) * 64]
                i_ = psT[:, bank: bank + NDT * 128].bitcast(I32).rearrange(
                    "p (dt t) -> p dt t", dt=NDT)
                c.do("vector", "tensor_copy", o, i_, inc="Tev")
                Tstate["g"] += 1
                if tb == NTB - 1:
                    c.put(("xnT", b))

            def S0_chunk(b, kc):
                if kc == 0:
                    c.wait("tensor", "E", c.cnt["E"])
                    if b > 0:
                        c.wait("tensor", "sG", c.cnt["sG"])
                        c.wait("tensor", "hact", c.cnt["hact"])
                base = c.at(("xnTbase", b), "Tev")
                c.wait("tensor", "Tev", base + 4 * (kc + 1))
                for dt in range(NDT):
                    c.do("tensor", "matmul",
                         psS[:, kc * 512:(kc + 1) * 512],
                         xnT_sb[:, dt * 2048: dt * 2048 + 128],
                         xnT_sb[:, dt * 2048 + kc * 512: dt * 2048 + (kc + 1) * 512],
                         start=(dt == 0), stop=(dt == NDT - 1),
                         inc="S" if (dt == NDT - 1 and kc == 3) else None)

            def tr_group(b, qq, g2):
                grp = Tstate["g"]
                bank = (grp % 2) * 1024
                e0 = (qq % 2) * 2048
                c.wait("tensor", "Tev", grp - 1)
                if g2 == 0:
                    c.wait("tensor", "E", c.at(("E", b, qq), "E"))
                for i in range(8):
                    kt = g2 * 8 + i
                    c.do("tensor", "transpose",
                         psT[:, bank + i * 128: bank + (i + 1) * 128],
                         Ebuf[:, e0 + kt * 128: e0 + (kt + 1) * 128],
                         identt[:], inc="Ttr" if i == 7 else None)
                if g2 == 1:
                    c.put(("Etr", b, qq))
                    Ehist.append(("Etr", b, qq))
                return bank

            def tr_evac(b, qq, g2, bank):
                c.wait("vector", "Ttr", c.cnt["Ttr"])
                c.do("vector", "tensor_copy",
                     aT[:, g2 * 1024:(g2 + 1) * 1024].bitcast(I32),
                     psT[:, bank: bank + 1024].bitcast(I32), inc="Tev")
                c.put(("aTev", b, qq, g2))
                Tstate["g"] += 1

            def softmax_q(b, q, embed_tr):
                if q == 0 and b > 0:
                    c.do("scalar", "activation", lnt[:, 0:1], identt[:, 0:1],
                         AF.Exp)
                c.wait("vector", "S", c.cnt["S"])
                if q >= 2:
                    c.wait("vector", "sevc", c.cnt["sevc"])
                c.do("vector", "tensor_reduce", nmx[:, q % 2: q % 2 + 1],
                     psS[:, 0:2048], AX.X, ALU.max, negate=True, inc="mx")
                if embed_tr:
                    bankB = tr_group(b, q - 1, 1)
                    tr_evac(b, q - 1, 1, bankB)
                c.wait("scalar", "mx", c.cnt["mx"])
                prev_same = [k for k in Ehist if k[2] % 2 == q % 2]
                if prev_same:
                    c.wait("scalar", "Ttr", c.at(prev_same[-1], "Ttr"))
                if q >= 2:
                    c.wait("scalar", "rec", c.cnt["rec"] - 1)
                c.do("scalar", "activation",
                     Ebuf[:, (q % 2) * 2048:(q % 2 + 1) * 2048],
                     psS[:, 0:2048], AF.Exp,
                     bias=nmx[:, q % 2: q % 2 + 1],
                     accum_out=sume[:, q % 2: q % 2 + 1], inc="E")
                c.wait("scalar", "E", c.cnt["E"])
                c.do("scalar", "copy", bnst[:, q % 2: q % 2 + 1],
                     sume[:, q % 2: q % 2 + 1], inc="E")
                c.wait("vector", "E", c.cnt["E"])
                c.do("vector", "reciprocal", recq[:, q % 2: q % 2 + 1],
                     bnst[:, q % 2: q % 2 + 1], inc="rec")
                c.put(("E", b, q))
                c.put(("rec", b, q))

            def attn_phase(b, leftover=None):
                lo = leftover if leftover is not None else []
                assert ("xnT", b) in c.stash

                for q in range(NTB + 1):
                    qq = q - 1
                    conv_out_step()
                    conv_wd_step(2)
                    if q == 0 or q == NTB:
                        conv_cvt_step(3)
                    if q < NTB:
                        if q >= 1:
                            c.wait("tensor", "E", c.cnt["E"])
                            for dt in range(NDT):
                                for kc in range(4):
                                    c.do("tensor", "matmul",
                                         psS[:, kc * 512:(kc + 1) * 512],
                                         xnT_sb[:, dt * 2048 + q * 128: dt * 2048 + (q + 1) * 128],
                                         xnT_sb[:, dt * 2048 + kc * 512: dt * 2048 + (kc + 1) * 512],
                                         start=(dt == 0), stop=(dt == NDT - 1),
                                         inc="S" if (dt == NDT - 1 and kc == 3) else None)
                    if q >= 1:
                        bankA = tr_group(b, qq, 0)
                        tr_evac(b, qq, 0, bankA)
                    if q < NTB and q >= 1:
                        softmax_q(b, q, embed_tr=True)
                        conv_cvt_step(3)
                        for _ in range(2):
                            if lo:
                                lo.pop(0)()
                    if q >= 1:
                        if q == NTB:
                            bankB = tr_group(b, qq, 1)
                            tr_evac(b, qq, 1, bankB)
                        c.wait("tensor", "sevc", c.cnt["sevc"])
                        if b > 0 and qq == 0:
                            c.wait("tensor", "gev", c.at(("ffn", b - 1), "gev"))
                        for g2 in range(2):
                            c.wait("tensor", "Tev", c.at(("aTev", b, qq, g2), "Tev"))
                            for i in range(8):
                                kt = g2 * 8 + i
                                for dh in range(2):
                                    c.do("tensor", "matmul",
                                         psO[:, dh * 512:(dh + 1) * 512],
                                         aT[:, kt * 128:(kt + 1) * 128],
                                         xn_sb[:, kt * 1024 + dh * 512:
                                               kt * 1024 + (dh + 1) * 512],
                                         start=(kt == 0), stop=(kt == 15),
                                         inc="O" if (kt == 15 and dh == 1) else None)
                        c.wait("scalar", "O", c.cnt["O"])
                        c.wait("scalar", "rec", c.at(("rec", b, qq), "rec"))
                        spq = ("spl", b, qq - 2)
                        if spq in spl.items:
                            sem, v = spl.items[spq]
                            c.wait("scalar", sem, v)
                        elif qq < 2 and b > 0:
                            sem, v = spl.items[("spl", b - 1, qq - 2 + NTB)]
                            c.wait("scalar", sem, v)
                        c.do("scalar", "activation",
                             sqb[:, (qq % 2) * 1024:(qq % 2 + 1) * 1024],
                             psO[:, 0:1024], AF.Copy,
                             scale=recq[:, qq % 2: qq % 2 + 1], inc="sevc")
                        c.wait("scalar", "sevc", c.cnt["sevc"])
                        spl.load("scalar", ("spl", b, qq),
                                 lambda s: s_scr[b, qq * 128:(qq + 1) * 128, :],
                                 sqb[:, (qq % 2) * 1024:(qq % 2 + 1) * 1024])
                        if qq == 12:
                            load_chunk(b, 0)
                            load_chunk(b, 1)
                while conv_state["done"] < N_CONV or conv_state["out"] < 16:
                    conv_cvt_step(1)
                    conv_out_step()
                conv_wd_step(NHT)
                c.put(("attn", b))

            wg3 = wg_scr.rearrange("(dt p) hh -> p dt hh", p=128)
            wu3 = wu_scr.rearrange("(dt p) hh -> p dt hh", p=128)

            def load_chunk(b, lidx):
                if lidx >= NTSB * 12:
                    return
                hp = lidx % 12
                if b == 0 and lidx == 0:
                    conv_wait("scalar")
                    conv_wait("sync")
                wgs.load("scalar", ("wg", b, lidx),
                         lambda s: wch[:, s * 2048:(s + 1) * 2048].rearrange(
                             "p (dt h) -> p dt h", h=256),
                         wg3[:, :, hp * 256:(hp + 1) * 256])
                wus.load("sync", ("wu", b, lidx),
                         lambda s: wch[:, (2 + s) * 2048:(3 + s) * 2048].rearrange(
                             "p (dt h) -> p dt h", h=256),
                         wu3[:, :, hp * 256:(hp + 1) * 256])

            def ffn_phase(b, bg_steps=None, epi_tiles=None):
                bg = list(bg_steps) if bg_steps else []
                epi = list(epi_tiles) if epi_tiles else []
                ybank_guard = {"psO": 0, "psU": 0}
                finals = []

                def queue_final(j, tbp, tb):
                    tbg = j * 4 + tbp * 2 + tb
                    gi = b * NTB + tbg
                    par = gi % 2
                    gev_need = c.at(("gevp", b, j, tbp), "gev")

                    def do_loads():
                        spl.wait_item("sync", ("spl", b, tbg))
                        sslot = srs.load("sync", ("srl", gi),
                                         lambda s: srlb[:, s * 1024:(s + 1) * 1024],
                                         s_scr[b, tbg * 128:(tbg + 1) * 128, :])
                        srs.consumed(sslot, "sgm", gi + 1)
                        xslot = xfs.load("sync", ("xfl", gi),
                                         lambda s: xfbuf[:, s * 1024:(s + 1) * 1024],
                                         x_hbm[b, tbg * 128:(tbg + 1) * 128, :])
                        xfs.consumed(xslot, "oadd", 2 * (gi + 1))
                        finals.insert(0, do_mul)

                    def do_mul():
                        c.wait("gpsimd", "gev", gev_need)
                        srs.wait_item("gpsimd", ("srl", gi))
                        if gi >= 1:
                            c.wait("gpsimd", "oadd", 2 * gi)
                        c.do("gpsimd", "tensor_tensor",
                             sgbuf[:, 0:1024],
                             srlb[:, par * 1024:(par + 1) * 1024],
                             gbuf[:, (gi % 4) * 1024:(gi % 4 + 1) * 1024],
                             ALU.mult, inc="sgm")
                        sgm_need = c.cnt["sgm"]

                        def do_adds():
                            oadds = []
                            for half in range(2):
                                c.wait("vector", "sgm", sgm_need)
                                xfs.wait_item("vector", ("xfl", gi))
                                osem = f"ost{half}"
                                c.wait("vector", osem, c.cnt[osem])
                                c.do("vector", "tensor_tensor",
                                     outbuf[:, half * 512:(half + 1) * 512],
                                     sgbuf[:, half * 512:(half + 1) * 512],
                                     xfbuf[:, par * 1024 + half * 512:
                                           par * 1024 + (half + 1) * 512],
                                     ALU.add, inc="oadd")
                                oadds.append(c.cnt["oadd"])

                            def do_stores():
                                for half in range(2):
                                    osem = f"ost{half}"
                                    c.wait("scalar", "oadd", oadds[half])
                                    c.dma("scalar",
                                          out_hbm[b, tbg * 128:(tbg + 1) * 128,
                                                  half * 512:(half + 1) * 512],
                                          outbuf[:, half * 512:(half + 1) * 512],
                                          osem)
                            finals.insert(0, do_stores)
                        finals.insert(0, do_adds)
                    finals.append(do_loads)

                def drain_final(n=1):
                    for _ in range(n):
                        if finals:
                            finals.pop(0)()

                for j in range(NTSB):
                    for hp in range(12):
                        gidx = j * 12 + hp
                        buf = gidx % 2
                        for hh in range(2):
                            ht = hp * 2 + hh
                            c.wait("tensor", "sG", c.cnt["sG"] - 1)
                            c.wait("tensor", "hact", c.cnt["hact"] - 1)
                            if j == 0 and ht == 0:
                                c.wait("tensor", "E", c.at(("attn", b), "E"))
                            wgs.wait_item("tensor", ("wg", b, gidx))
                            wus.wait_item("tensor", ("wu", b, gidx))
                            hb = (ht % 2) * 512
                            for dt in range(NDT):
                                c.do("tensor", "matmul", psG[:, hb: hb + 512],
                                     wch[:, buf * 2048 + dt * 256 + hh * 128:
                                         buf * 2048 + dt * 256 + (hh + 1) * 128],
                                     xnT_sb[:, dt * 2048 + j * 512: dt * 2048 + (j + 1) * 512],
                                     start=(dt == 0), stop=(dt == NDT - 1),
                                     inc="G" if dt == NDT - 1 else None)
                            if ht == 0:
                                c.wait("tensor", "gev", ybank_guard["psU"])
                            for dt in range(NDT):
                                c.do("tensor", "matmul", psU[:, hb: hb + 512],
                                     wch[:, (2 + buf) * 2048 + dt * 256 + hh * 128:
                                         (2 + buf) * 2048 + dt * 256 + (hh + 1) * 128],
                                     xnT_sb[:, dt * 2048 + j * 512: dt * 2048 + (j + 1) * 512],
                                     start=(dt == 0), stop=(dt == NDT - 1),
                                     inc="U" if dt == NDT - 1 else None)
                            if j == 0 and hp == 0 and hh == 0:
                                c.do("scalar", "activation", lnt[:, 0:1],
                                     identt[:, 0:1], AF.Silu)
                            c.wait("scalar", "G", c.cnt["G"])
                            c.wait("scalar", "hact", c.cnt["hact"])
                            c.do("scalar", "activation",
                                 sGb[:, 0:512],
                                 psG[:, hb: hb + 512], AF.Silu, scale=SCALE_G, inc="sG")
                            c.wait("vector", "U", c.cnt["U"])
                            c.wait("vector", "sG", c.cnt["sG"])
                            if b > 0 or j > 0:
                                key = ("Ytot", b, j - 1) if j > 0 else ("Ytot", b - 1, NTSB - 1)
                                c.wait("vector", "Y", c.at(key, "Y"))
                            if b == 0 and j == 0 and ht == 0:
                                c.wait("vector", "cvt", c.at("conv_all", "cvt"))
                                c.wait("vector", "cvtw", NHT)
                            c.do("vector", "scalar_tensor_tensor",
                                 hact[:, ht * 512:(ht + 1) * 512],
                                 psU[:, hb: hb + 512], 1.0,
                                 sGb[:, 0:512],
                                 ALU.mult, ALU.mult, inc="hact")
                            c.put(("hactd", b, j, ht))
                        c.put(("wchG", b, gidx))
                        wgs.consumed(buf, "G", c.at(("wchG", b, gidx), "G"))
                        wus.consumed(buf, "U", c.at(("wchG", b, gidx), "U"))
                        load_chunk(b, gidx + 2)
                        drain_final(2)
                    for tbp in range(2):
                        for dh in range(2):
                            borrow = (dh == 1 and j != NTSB - 1)
                            ybank = psU if borrow else psY
                            guard_key = "psU" if borrow else "psO"
                            c.wait("tensor", "gev", ybank_guard[guard_key])
                            if j == 0 and tbp == 0 and dh == 0:
                                c.wait("tensor", "sevc", c.at(("attn", b), "sevc"))
                                if b == 0:
                                    c.wait("tensor", "cvtw", NHT)
                            if borrow and tbp == 0:
                                c.wait("tensor", "hact",
                                       c.at(("hactd", b, j, NHT - 1), "hact"))
                            for ht in range(NHT):
                                if tbp == 0 and dh == 0:
                                    c.wait("tensor", "hact",
                                           c.at(("hactd", b, j, ht), "hact"))
                                for tb in range(2):
                                    c.do("tensor", "matmul",
                                         ybank[:, tb * 512:(tb + 1) * 512],
                                         hact[:, ht * 512 + (tbp * 2 + tb) * 128:
                                              ht * 512 + (tbp * 2 + tb + 1) * 128],
                                         wdT_sb[:, ht * 1024 + dh * 512:
                                                ht * 1024 + (dh + 1) * 512],
                                         start=(ht == 0), stop=(ht == NHT - 1),
                                         inc="Y" if (ht == NHT - 1 and tb == 1) else None)
                            if j == NTSB - 1 and epi:
                                for _ in range(4):
                                    if epi:
                                        epi.pop(0)()
                            c.wait("scalar", "Y", c.cnt["Y"])
                            for tb in range(2):
                                tbg = j * 4 + tbp * 2 + tb
                                gi = b * NTB + tbg
                                if dh == 0 and gi >= 4:
                                    c.wait("scalar", "sgm", gi - 3)
                                c.do("scalar", "activation",
                                     gbuf[:, (gi % 4) * 1024 + dh * 512:
                                          (gi % 4) * 1024 + (dh + 1) * 512],
                                     ybank[:, tb * 512:(tb + 1) * 512], AF.Silu,
                                     scale=SCALE_Y, inc="gev")
                            ybank_guard[guard_key] = c.cnt["gev"]
                            if j == NTSB - 1 and tbp == 1 and dh == 1:
                                while epi:
                                    epi.pop(0)()
                            if b == NB - 1 and j == NTSB - 1:
                                drain_final(3)
                            for _ in range(3):
                                if bg:
                                    bg.pop(0)()
                        c.put(("gevp", b, j, tbp))
                        for tb in range(2):
                            queue_final(j, tbp, tb)
                    c.put(("Ytot", b, j))
                c.put(("ffn", b))
                return finals

            ln_phase(0)
            conversions_in()
            attn_phase(0)
            lo1 = ffn_phase(0, bg_steps=ln_phase(1, as_steps=True),
                            epi_tiles=[
                                (lambda tb=tb: (
                                    xnT_tile(1, tb),
                                    S0_chunk(1, tb // 4) if tb % 4 == 3 else None))
                                for tb in range(NTB)]
                            + [lambda: softmax_q(1, 0, embed_tr=False)])
            attn_phase(1, leftover=lo1)
            rest = ffn_phase(1)
            while rest:
                rest.pop(0)()

            if "s" in dbg:
                for b in range(NB):
                    for qq in range(NTB):
                        spl.wait_item("gpsimd", ("spl", b, qq))
                c.dma("gpsimd", dbg["s"][:, :, :], s_scr[:, :, :], "dbgs")
            if "xn" in dbg:
                for tb in range(NTB):
                    c.dma("gpsimd", dbg["xn"][NB - 1, tb * 128:(tb + 1) * 128, :],
                          xn_sb[:, tb * 1024:(tb + 1) * 1024], "dbgs")
            if "xnT" in dbg:
                for dt in range(NDT):
                    c.dma("gpsimd", dbg["xnT"][NB - 1, dt * 128:(dt + 1) * 128, :],
                          xnT_sb[:, dt * 2048:(dt + 1) * 2048], "dbgs")

            c.wait("scalar", "ost0", c.cnt["ost0"])
            c.wait("scalar", "ost1", c.cnt["ost1"])
            if dbg:
                c.wait("gpsimd", "dbgs", c.cnt["dbgs"])

        base = {"tensor": None, "vector": None, "scalar": None,
                "sync": None, "gpsimd": None}

        @block.tensor
        def _(tensor):
            walk("tensor", {**base, "tensor": tensor})

        @block.vector
        def _(vector):
            walk("vector", {**base, "vector": vector})

        @block.scalar
        def _(scalar):
            walk("scalar", {**base, "scalar": scalar})

        @block.sync
        def _(sync):
            walk("sync", {**base, "sync": sync})

        @block.gpsimd
        def _(gpsimd):
            walk("gpsimd", {**base, "gpsimd": gpsimd})

    return nc


def kernel(L0_attn_x, L0_ffn_w_gate, L0_ffn_w_up, L0_ffn_w_down):
    x = np.ascontiguousarray(np.asarray(L0_attn_x, dtype=np.float32))
    wgT = np.ascontiguousarray(np.asarray(L0_ffn_w_gate, dtype=np.int32).T)
    wuT = np.ascontiguousarray(np.asarray(L0_ffn_w_up, dtype=np.int32).T)
    wdT = np.ascontiguousarray(np.asarray(L0_ffn_w_down, dtype=np.int32).T)
    ident = np.eye(128, dtype=ml_dtypes.bfloat16)

    nc = build_program()
    in_maps = []
    for core in range(8):
        in_maps.append({
            "x": x[core * NB:(core + 1) * NB],
            "wgT": wgT, "wuT": wuT, "wdT": wdT,
            "ident": ident,
        })
    res = run_bass_kernel_spmd(nc, in_maps, core_ids=list(range(8)))
    out = np.concatenate([res.results[i]["out"] for i in range(8)], axis=0)
    if DEBUG_OUTPUTS:
        kernel.debug = [{k: np.asarray(res.results[i]["dbg_" + k])
                         for k in DEBUG_OUTPUTS} for i in range(8)]
    return out


# version 38
# speedup vs baseline: 1.1715x; 1.1715x over previous
import contextlib

import numpy as np
import ml_dtypes
import concourse.bass as bass
import concourse.mybir as mybir
from concourse.bass_utils import run_bass_kernel_spmd

F32 = mybir.dt.float32
BF16 = mybir.dt.bfloat16
I32 = mybir.dt.int32
AF = mybir.ActivationFunctionType
ALU = mybir.AluOpType
AX = mybir.AxisListType

NB = 2
T = 2048
D = 1024
H = 3072
NTB = T // 128
NDT = D // 128
NHT = H // 128
TSB = 512
NTSB = T // TSB
EPS = 1e-5
INV127 = float(np.float32(1.0) / np.float32(127.0))
SCALE_G = INV127
SCALE_Y = INV127 * INV127

DEBUG_OUTPUTS = []


class Ctx:

    def __init__(self, live, proxies, sems, stash):
        self.live = live
        self.prox = proxies
        self.sems = sems
        self.cnt = {k: 0 for k in sems}
        self.stash = stash

    def wait(self, eng, sem, val):
        if val is None or val <= 0:
            return
        if self.live == eng:
            self.prox[eng].wait_ge(self.sems[sem], val)

    def do(self, eng, method, *a, inc=None, incval=1, **kw):
        if self.live == eng:
            ins = getattr(self.prox[eng], method)(*a, **kw)
            if inc is not None:
                ins.then_inc(self.sems[inc], incval)
        if inc is not None:
            self.cnt[inc] += incval

    def dma(self, eng, out, in_, sem, **kw):
        self.do(eng, "dma_start", inc=sem, incval=16, out=out, in_=in_, **kw)

    def put(self, key):
        if key not in self.stash:
            self.stash[key] = dict(self.cnt)

    def at(self, key, sem):
        return self.stash[key][sem]


class DStream:

    def __init__(self, c: Ctx, name, n):
        self.c, self.name, self.n = c, name, n
        self.guard = [None] * n
        self.items = {}
        self.i = 0

    @staticmethod
    def sem_names(name, n):
        return [f"{name}{i}" for i in range(n)]

    def load(self, eng, key, out_of_slot, in_ap, **kw):
        s = self.i % self.n
        self.i += 1
        g = self.guard[s]
        if g:
            self.c.wait(eng, g[0], g[1])
        sem = f"{self.name}{s}"
        self.c.dma(eng, out_of_slot(s), in_ap, sem, **kw)
        self.items[key] = (sem, self.c.cnt[sem])
        return s

    def consumed(self, slot, sem, val):
        self.guard[slot] = (sem, val)

    def wait_item(self, eng, key):
        sem, v = self.items[key]
        self.c.wait(eng, sem, v)

    def totals(self):
        out = {}
        for s in range(self.n):
            out[f"{self.name}{s}"] = self.c.cnt[f"{self.name}{s}"]
        return out


def build_program():
    nc = bass.Bass("TRN2", target_bir_lowering=False)

    x_hbm = nc.dram_tensor("x", [NB, T, D], F32, kind="ExternalInput").ap()
    wgT_hbm = nc.dram_tensor("wgT", [D, H], I32, kind="ExternalInput").ap()
    wuT_hbm = nc.dram_tensor("wuT", [D, H], I32, kind="ExternalInput").ap()
    wdT_hbm = nc.dram_tensor("wdT", [H, D], I32, kind="ExternalInput").ap()
    ident_hbm = nc.dram_tensor("ident", [128, 128], BF16, kind="ExternalInput").ap()
    out_hbm = nc.dram_tensor("out", [NB, T, D], F32, kind="ExternalOutput").ap()

    wg_scr = nc.dram_tensor("wg_scr", [D, H], BF16).ap()
    wu_scr = nc.dram_tensor("wu_scr", [D, H], BF16).ap()
    s_scr = nc.dram_tensor("s_scr", [NB, T, D], BF16).ap()

    dbg = {}
    for name in DEBUG_OUTPUTS:
        shp = {"xn": [NB, T, D], "xnT": [NB, D, T], "s": [NB, T, D]}[name]
        dt = {"xn": F32, "xnT": F32, "s": BF16}[name]
        dbg[name] = nc.dram_tensor("dbg_" + name, shp, dt, kind="ExternalOutput").ap()

    sem_names = (
        ["idld", "cvt", "cvtw", "bn", "rstd", "xnw",
         "Ttr", "Tev", "S", "mx", "E", "rec", "O", "sevc",
         "G", "U", "sG", "hact", "Y", "gev", "sgm", "oadd", "dbgs"]
        + DStream.sem_names("xld", 3)
        + DStream.sem_names("cvc", 2)
        + DStream.sem_names("cwc", 2)
        + DStream.sem_names("cvo", 2)
        + DStream.sem_names("spl", 2)
        + DStream.sem_names("srl", 2)
        + DStream.sem_names("xfl", 2)
        + DStream.sem_names("wgl", 2)
        + DStream.sem_names("wul", 2)
        + DStream.sem_names("ost", 2)
    )

    shared_stash = {}

    with contextlib.ExitStack() as stack:
        ent = stack.enter_context
        wdT_sb = ent(nc.sbuf_tensor("wdT_sb", [128, NHT * 1024], BF16))
        xn_sb = ent(nc.sbuf_tensor("xn_sb", [128, NTB * 1024], BF16))
        xnT_sb = ent(nc.sbuf_tensor("xnT_sb", [128, NDT * 2048], BF16))
        hact = ent(nc.sbuf_tensor("hact", [128, NHT * 512], BF16))
        Ebuf = ent(nc.sbuf_tensor("Ebuf", [128, 2 * 2048], BF16))
        aT = ent(nc.sbuf_tensor("aT", [128, 2048], BF16))
        wch = ent(nc.sbuf_tensor("wch", [128, 4 * 2048], BF16))
        xbuf = ent(nc.sbuf_tensor("xbuf", [128, 3 * 1024], F32))
        xfbuf = ent(nc.sbuf_tensor("xfbuf", [128, 2 * 1024], F32))
        outbuf = ent(nc.sbuf_tensor("outbuf", [128, 2 * 512], F32))
        sqb = ent(nc.sbuf_tensor("sqb", [128, 2 * 1024], BF16))
        srlb = ent(nc.sbuf_tensor("srlb", [128, 2 * 1024], BF16))
        gbuf = ent(nc.sbuf_tensor("gbuf", [128, 4 * 1024], BF16))
        sgbuf = ent(nc.sbuf_tensor("sgbuf", [128, 1024], BF16))
        sGb = ent(nc.sbuf_tensor("sGb", [128, 512], BF16))
        identt = ent(nc.sbuf_tensor("identt", [128, 128], BF16))
        mvAll = ent(nc.sbuf_tensor("mvAll", [128, 32], F32))
        bnst = ent(nc.sbuf_tensor("bnst", [128, 12], F32))
        veps = ent(nc.sbuf_tensor("veps", [128, 16], F32))
        lnt = ent(nc.sbuf_tensor("lnt", [128, 16], F32))
        rstd16 = ent(nc.sbuf_tensor("rstd16", [128, 16], F32))
        nmr16 = ent(nc.sbuf_tensor("nmr16", [128, 16], F32))
        nmx = ent(nc.sbuf_tensor("nmx", [128, 2], F32))
        sume = ent(nc.sbuf_tensor("sume", [128, 2], F32))
        recq = ent(nc.sbuf_tensor("recq", [128, 2], F32))
        psT = ent(nc.psum_tensor("psT", [128, 2048], BF16))
        psF = ent(nc.psum_tensor("psF", [128, 3072], F32))
        sems = {n: ent(nc.semaphore(n)) for n in sem_names}
        block = ent(nc.Block())

        def walk(live, proxies):
            c = Ctx(live, proxies, sems, shared_stash)
            mv2 = mvAll[:].rearrange("p (t two) -> p t two", two=2)
            mv_mean = mv2[:, :, 0]
            mv_var = mv2[:, :, 1]

            psS = psF[:, 0:2048]
            psG = psF[:, 0:1024]
            psU = psF[:, 1024:2048]
            psO = psF[:, 2048:3072]
            psY = psO

            xs = DStream(c, "xld", 3)
            spl = DStream(c, "spl", 2)
            srs = DStream(c, "srl", 2)
            xfs = DStream(c, "xfl", 2)
            wgs = DStream(c, "wgl", 2)
            wus = DStream(c, "wul", 2)
            obs = DStream(c, "ost", 2)
            Tstate = {"g": 0}
            Ehist = []

            c.dma("sync", identt[:], ident_hbm, "idld")

            stg_i32 = [hact[:, 0:6144].bitcast(I32), hact[:, 6144:12288].bitcast(I32)]
            cw_bf = [wch[:, 0:3072], wch[:, 3072:6144]]
            N_CONV = 16
            conv_state = {"loads": [], "done": 0, "out": 0}
            wd_state = {"done": 0, "loaded": 0, "lsem": []}

            def conversions_in():
                for k in range(16):
                    par = k % 2
                    if k >= 2:
                        c.wait("gpsimd", "cvt", k - 1)
                    src = (wgT_hbm, wuT_hbm)[k // 8]
                    ch = k % 8
                    c.dma("gpsimd", stg_i32[par],
                          src[ch * 128:(ch + 1) * 128, :], f"cvc{par}")
                    conv_state["loads"].append((f"cvc{par}", c.cnt[f"cvc{par}"]))

            def conv_cvt_step(n):
                for _ in range(n):
                    k = conv_state["done"]
                    if k >= N_CONV:
                        return
                    if 2 <= k < 16 and conv_state["out"] < k - 1:
                        return
                    conv_state["done"] += 1
                    par = k % 2
                    sem, v = conv_state["loads"][k]
                    c.wait("vector", sem, v)
                    if k >= 2:
                        c.wait("vector", f"cvo{par}", 16 * ((k - par) // 2))
                    c.do("vector", "tensor_copy", cw_bf[par], stg_i32[par],
                         inc="cvt")
                    if conv_state["done"] == N_CONV:
                        c.put("conv_all")

            stg_wd = [srlb[:, 0:2048].bitcast(I32), gbuf[:, 0:2048].bitcast(I32)]

            def conv_wd_step(n):
                for _ in range(n):
                    ht = wd_state["done"]
                    if ht >= NHT:
                        return
                    wd_state["done"] += 1
                    while wd_state["loaded"] <= min(ht + 1, NHT - 1):
                        lt = wd_state["loaded"]
                        wd_state["loaded"] += 1
                        sl = lt % 2
                        if lt >= 2:
                            c.wait("gpsimd", "cvtw", lt - 1)
                        c.dma("gpsimd", stg_wd[sl],
                              wdT_hbm[lt * 128:(lt + 1) * 128, :], f"cwc{sl}")
                        wd_state["lsem"].append((f"cwc{sl}", c.cnt[f"cwc{sl}"]))
                    sem, v = wd_state["lsem"][ht]
                    c.wait("gpsimd", sem, v)
                    c.do("gpsimd", "tensor_copy",
                         wdT_sb[:, ht * 1024:(ht + 1) * 1024],
                         stg_wd[ht % 2], inc="cvtw")

            def conv_out_step():
                while conv_state["out"] < min(conv_state["done"], 16):
                    k = conv_state["out"]
                    conv_state["out"] += 1
                    par = k % 2
                    c.wait("scalar", "cvt", k + 1)
                    dst = (wg_scr, wu_scr)[k // 8]
                    ch = k % 8
                    c.dma("scalar", dst[ch * 128:(ch + 1) * 128, :], cw_bf[par],
                          f"cvo{par}")
                    if conv_state["out"] == 16:
                        c.put("conv_done")

            def conv_wait(eng):
                c.wait(eng, "cvo0", c.at("conv_done", "cvo0"))
                c.wait(eng, "cvo1", c.at("conv_done", "cvo1"))

            def ln_phase(b, as_steps=False):
                steps = []

                def step(fn):
                    if as_steps:
                        steps.append(fn)
                    else:
                        fn()

                def x_eng(tb):
                    return "scalar" if (b == 0 and tb % 2 == 1) else "sync"

                def p1_tile(tb):
                    par = xs.load(x_eng(tb), ("p1", b, tb),
                                  lambda s: xbuf[:, s * 1024:(s + 1) * 1024],
                                  x_hbm[b, tb * 128:(tb + 1) * 128, :])
                    xs.wait_item("vector", ("p1", b, tb))
                    c.wait("vector", "bn", c.cnt["bn"])
                    c.do("vector", "bn_stats", bnst[:, 0:6],
                         xbuf[:, par * 1024: par * 1024 + 512], inc="bn")
                    c.wait("vector", "bn", c.cnt["bn"])
                    c.do("vector", "bn_stats", bnst[:, 6:12],
                         xbuf[:, par * 1024 + 512: par * 1024 + 1024], inc="bn")
                    c.wait("vector", "bn", c.cnt["bn"])
                    c.do("vector", "bn_aggr", mvAll[:, 2 * tb: 2 * tb + 2],
                         bnst[:, 0:12], inc="bn")
                    xs.consumed(par, "bn", c.cnt["bn"])

                nmr_holder = [None]

                def rstd_block():
                    c.wait("vector", "bn", c.cnt["bn"])
                    c.do("vector", "tensor_scalar_add", veps[:, 0:16], mv_var,
                         EPS, inc="bn")
                    c.wait("scalar", "bn", c.cnt["bn"])
                    c.do("scalar", "activation", lnt[:, 0:16], veps[:, 0:16],
                         AF.Ln, inc="rstd")
                    c.wait("scalar", "rstd", c.cnt["rstd"])
                    c.do("scalar", "activation", rstd16[:, 0:16], lnt[:, 0:16],
                         AF.Exp, scale=-0.5, inc="rstd")
                    c.wait("vector", "rstd", c.cnt["rstd"])
                    c.do("vector", "scalar_tensor_tensor", nmr16[:, 0:16],
                         mv_mean, -1.0, rstd16[:, 0:16], ALU.mult, ALU.mult,
                         inc="bn")
                    nmr_holder[0] = c.cnt["bn"]

                def p2_tile(tb):
                    eng = "gpsimd"
                    par = xs.load(x_eng(tb), ("p2", b, tb),
                                  lambda s: xbuf[:, s * 1024:(s + 1) * 1024],
                                  x_hbm[b, tb * 128:(tb + 1) * 128, :])
                    xs.wait_item(eng, ("p2", b, tb))
                    c.wait(eng, "bn", nmr_holder[0])
                    if b >= 1 and tb == 0:
                        c.wait(eng, "O", c.at(("attn", b - 1), "O"))
                    c.do(eng, "tensor_scalar",
                         xn_sb[:, tb * 1024:(tb + 1) * 1024],
                         xbuf[:, par * 1024:(par + 1) * 1024],
                         rstd16[:, tb:tb + 1], nmr16[:, tb:tb + 1],
                         ALU.mult, ALU.add, inc="xnw")
                    xs.consumed(par, "xnw", c.cnt["xnw"])
                    if b == 0:
                        xnT_tile(b, tb)
                        if tb % 4 == 3:
                            S0_chunk(b, tb // 4)
                    if tb == NTB - 1:
                        c.put(("xnw", b))

                def rstd_half(lo, hi):
                    c.wait("vector", "bn", c.cnt["bn"])
                    c.do("vector", "tensor_scalar_add", veps[:, lo:hi],
                         mv_var[:, lo:hi], EPS, inc="bn")
                    c.wait("scalar", "bn", c.cnt["bn"])
                    c.do("scalar", "activation", lnt[:, lo:hi], veps[:, lo:hi],
                         AF.Ln, inc="rstd")
                    c.wait("scalar", "rstd", c.cnt["rstd"])
                    c.do("scalar", "activation", rstd16[:, lo:hi], lnt[:, lo:hi],
                         AF.Exp, scale=-0.5, inc="rstd")
                    c.wait("vector", "rstd", c.cnt["rstd"])
                    c.do("vector", "scalar_tensor_tensor", nmr16[:, lo:hi],
                         mv_mean[:, lo:hi], -1.0, rstd16[:, lo:hi],
                         ALU.mult, ALU.mult, inc="bn")
                    nmr_holder[0] = c.cnt["bn"]

                if b == 0 and not as_steps:
                    for tb in range(4):
                        p1_tile(tb)
                    rstd_half(0, 4)
                    for q4 in range(1, 4):
                        for tb in range(4):
                            p1_tile(4 * q4 + tb)
                            p2_tile(4 * (q4 - 1) + tb)
                        rstd_half(4 * q4, 4 * q4 + 4)
                    for tb in range(12, NTB):
                        p2_tile(tb)
                    softmax_q(0, 0, embed_tr=False)
                    return None
                for tb in range(NTB):
                    step(lambda tb=tb: p1_tile(tb))
                step(rstd_block)
                for tb in range(NTB):
                    step(lambda tb=tb: p2_tile(tb))
                if as_steps:
                    return steps

            def xnT_tile(b, tb):
                if tb == 0:
                    c.put(("xnTbase", b))
                grp = Tstate["g"]
                bank = (grp % 2) * 1024
                c.wait("tensor", "Tev", grp - 1)
                c.wait("tensor", "xnw", b * NTB + tb + 1)
                if grp == 0:
                    c.wait("tensor", "idld", 16)
                for dt in range(NDT):
                    c.do("tensor", "transpose",
                         psT[:, bank + dt * 128: bank + (dt + 1) * 128],
                         xn_sb[:, tb * 1024 + dt * 128: tb * 1024 + (dt + 1) * 128],
                         identt[:], inc="Ttr" if dt == NDT - 1 else None)
                c.wait("vector", "Ttr", c.cnt["Ttr"])
                if b >= 1 and tb == 0:
                    c.wait("vector", "G", c.cnt["G"])
                    c.wait("vector", "U", c.cnt["U"])
                o = xnT_sb[:].bitcast(I32).rearrange(
                    "p (dt t) -> p dt t", dt=NDT)[:, :, tb * 64:(tb + 1) * 64]
                i_ = psT[:, bank: bank + NDT * 128].bitcast(I32).rearrange(
                    "p (dt t) -> p dt t", dt=NDT)
                c.do("vector", "tensor_copy", o, i_, inc="Tev")
                Tstate["g"] += 1
                if tb == NTB - 1:
                    c.put(("xnT", b))

            def S0_chunk(b, kc):
                if kc == 0:
                    c.wait("tensor", "E", c.cnt["E"])
                    if b > 0:
                        c.wait("tensor", "sG", c.cnt["sG"])
                        c.wait("tensor", "hact", c.cnt["hact"])
                base = c.at(("xnTbase", b), "Tev")
                c.wait("tensor", "Tev", base + 4 * (kc + 1))
                for dt in range(NDT):
                    c.do("tensor", "matmul",
                         psS[:, kc * 512:(kc + 1) * 512],
                         xnT_sb[:, dt * 2048: dt * 2048 + 128],
                         xnT_sb[:, dt * 2048 + kc * 512: dt * 2048 + (kc + 1) * 512],
                         start=(dt == 0), stop=(dt == NDT - 1),
                         inc="S" if (dt == NDT - 1 and kc == 3) else None)

            def tr_group(b, qq, g2):
                grp = Tstate["g"]
                bank = (grp % 2) * 1024
                e0 = (qq % 2) * 2048
                c.wait("tensor", "Tev", grp - 1)
                if g2 == 0:
                    c.wait("tensor", "E", c.at(("E", b, qq), "E"))
                for i in range(8):
                    kt = g2 * 8 + i
                    c.do("tensor", "transpose",
                         psT[:, bank + i * 128: bank + (i + 1) * 128],
                         Ebuf[:, e0 + kt * 128: e0 + (kt + 1) * 128],
                         identt[:], inc="Ttr" if i == 7 else None)
                if g2 == 1:
                    c.put(("Etr", b, qq))
                    Ehist.append(("Etr", b, qq))
                return bank

            def tr_evac(b, qq, g2, bank):
                c.wait("vector", "Ttr", c.cnt["Ttr"])
                c.do("vector", "tensor_copy",
                     aT[:, g2 * 1024:(g2 + 1) * 1024].bitcast(I32),
                     psT[:, bank: bank + 1024].bitcast(I32), inc="Tev")
                c.put(("aTev", b, qq, g2))
                Tstate["g"] += 1

            def softmax_q(b, q, embed_tr):
                if q == 0 and b > 0:
                    c.do("scalar", "activation", lnt[:, 0:1], identt[:, 0:1],
                         AF.Exp)
                c.wait("vector", "S", c.cnt["S"])
                if q >= 2:
                    c.wait("vector", "sevc", c.cnt["sevc"])
                c.do("vector", "tensor_reduce", nmx[:, q % 2: q % 2 + 1],
                     psS[:, 0:2048], AX.X, ALU.max, negate=True, inc="mx")
                if embed_tr:
                    bankB = tr_group(b, q - 1, 1)
                    tr_evac(b, q - 1, 1, bankB)
                c.wait("scalar", "mx", c.cnt["mx"])
                prev_same = [k for k in Ehist if k[2] % 2 == q % 2]
                if prev_same:
                    c.wait("scalar", "Ttr", c.at(prev_same[-1], "Ttr"))
                if q >= 2:
                    c.wait("scalar", "rec", c.cnt["rec"] - 1)
                c.do("scalar", "activation",
                     Ebuf[:, (q % 2) * 2048:(q % 2 + 1) * 2048],
                     psS[:, 0:2048], AF.Exp,
                     bias=nmx[:, q % 2: q % 2 + 1],
                     accum_out=sume[:, q % 2: q % 2 + 1], inc="E")
                c.wait("scalar", "E", c.cnt["E"])
                c.do("scalar", "copy", bnst[:, q % 2: q % 2 + 1],
                     sume[:, q % 2: q % 2 + 1], inc="E")
                c.wait("vector", "E", c.cnt["E"])
                c.do("vector", "reciprocal", recq[:, q % 2: q % 2 + 1],
                     bnst[:, q % 2: q % 2 + 1], inc="rec")
                c.put(("E", b, q))
                c.put(("rec", b, q))

            def attn_phase(b, leftover=None):
                lo = leftover if leftover is not None else []
                assert ("xnT", b) in c.stash

                for q in range(NTB + 1):
                    qq = q - 1
                    conv_out_step()
                    conv_wd_step(2)
                    if q == 0 or q == NTB:
                        conv_cvt_step(3)
                    if q < NTB:
                        if q >= 1:
                            c.wait("tensor", "E", c.cnt["E"])
                            for dt in range(NDT):
                                for kc in range(4):
                                    c.do("tensor", "matmul",
                                         psS[:, kc * 512:(kc + 1) * 512],
                                         xnT_sb[:, dt * 2048 + q * 128: dt * 2048 + (q + 1) * 128],
                                         xnT_sb[:, dt * 2048 + kc * 512: dt * 2048 + (kc + 1) * 512],
                                         start=(dt == 0), stop=(dt == NDT - 1),
                                         inc="S" if (dt == NDT - 1 and kc == 3) else None)
                    if q >= 1:
                        bankA = tr_group(b, qq, 0)
                        tr_evac(b, qq, 0, bankA)
                    if q < NTB and q >= 1:
                        softmax_q(b, q, embed_tr=True)
                        conv_cvt_step(3)
                        for _ in range(2):
                            if lo:
                                lo.pop(0)()
                    if q >= 1:
                        if q == NTB:
                            bankB = tr_group(b, qq, 1)
                            tr_evac(b, qq, 1, bankB)
                        c.wait("tensor", "sevc", c.cnt["sevc"])
                        if b > 0 and qq == 0:
                            c.wait("tensor", "gev", c.at(("ffn", b - 1), "gev"))
                        for g2 in range(2):
                            c.wait("tensor", "Tev", c.at(("aTev", b, qq, g2), "Tev"))
                            for i in range(8):
                                kt = g2 * 8 + i
                                for dh in range(2):
                                    c.do("tensor", "matmul",
                                         psO[:, dh * 512:(dh + 1) * 512],
                                         aT[:, kt * 128:(kt + 1) * 128],
                                         xn_sb[:, kt * 1024 + dh * 512:
                                               kt * 1024 + (dh + 1) * 512],
                                         start=(kt == 0), stop=(kt == 15),
                                         inc="O" if (kt == 15 and dh == 1) else None)
                        c.wait("scalar", "O", c.cnt["O"])
                        c.wait("scalar", "rec", c.at(("rec", b, qq), "rec"))
                        spq = ("spl", b, qq - 2)
                        if spq in spl.items:
                            sem, v = spl.items[spq]
                            c.wait("scalar", sem, v)
                        elif qq < 2 and b > 0:
                            sem, v = spl.items[("spl", b - 1, qq - 2 + NTB)]
                            c.wait("scalar", sem, v)
                        c.do("scalar", "activation",
                             sqb[:, (qq % 2) * 1024:(qq % 2 + 1) * 1024],
                             psO[:, 0:1024], AF.Copy,
                             scale=recq[:, qq % 2: qq % 2 + 1], inc="sevc")
                        c.wait("scalar", "sevc", c.cnt["sevc"])
                        spl.load("scalar", ("spl", b, qq),
                                 lambda s: s_scr[b, qq * 128:(qq + 1) * 128, :],
                                 sqb[:, (qq % 2) * 1024:(qq % 2 + 1) * 1024])
                        if qq == 12:
                            load_chunk(b, 0)
                            load_chunk(b, 1)
                while conv_state["done"] < N_CONV or conv_state["out"] < 16:
                    conv_cvt_step(1)
                    conv_out_step()
                conv_wd_step(NHT)
                c.put(("attn", b))

            wg3 = wg_scr.rearrange("(dt p) hh -> p dt hh", p=128)
            wu3 = wu_scr.rearrange("(dt p) hh -> p dt hh", p=128)

            def load_chunk(b, lidx):
                if lidx >= NTSB * 12:
                    return
                hp = lidx % 12
                if b == 0 and lidx == 0:
                    conv_wait("scalar")
                    conv_wait("sync")
                wgs.load("scalar", ("wg", b, lidx),
                         lambda s: wch[:, s * 2048:(s + 1) * 2048].rearrange(
                             "p (dt h) -> p dt h", h=256),
                         wg3[:, :, hp * 256:(hp + 1) * 256])
                wus.load("sync", ("wu", b, lidx),
                         lambda s: wch[:, (2 + s) * 2048:(3 + s) * 2048].rearrange(
                             "p (dt h) -> p dt h", h=256),
                         wu3[:, :, hp * 256:(hp + 1) * 256])

            def ffn_phase(b, bg_steps=None, epi_tiles=None):
                bg = list(bg_steps) if bg_steps else []
                epi = list(epi_tiles) if epi_tiles else []
                ybank_guard = {"psO": 0, "psU": 0}
                finals = []

                def queue_final(j, tbp, tb):
                    tbg = j * 4 + tbp * 2 + tb
                    gi = b * NTB + tbg
                    par = gi % 2
                    gev_need = c.at(("gevp", b, j, tbp), "gev")

                    def do_loads():
                        spl.wait_item("sync", ("spl", b, tbg))
                        sslot = srs.load("sync", ("srl", gi),
                                         lambda s: srlb[:, s * 1024:(s + 1) * 1024],
                                         s_scr[b, tbg * 128:(tbg + 1) * 128, :])
                        srs.consumed(sslot, "sgm", gi + 1)
                        xslot = xfs.load("sync", ("xfl", gi),
                                         lambda s: xfbuf[:, s * 1024:(s + 1) * 1024],
                                         x_hbm[b, tbg * 128:(tbg + 1) * 128, :])
                        xfs.consumed(xslot, "oadd", 2 * (gi + 1))
                        finals.insert(0, do_mul)

                    def do_mul():
                        meng = "vector" if gi >= 2 * NTB - 4 else "gpsimd"
                        c.wait(meng, "gev", gev_need)
                        srs.wait_item(meng, ("srl", gi))
                        if gi >= 1:
                            c.wait(meng, "oadd", 2 * gi)
                        c.do(meng, "tensor_tensor",
                             sgbuf[:, 0:1024],
                             srlb[:, par * 1024:(par + 1) * 1024],
                             gbuf[:, (gi % 4) * 1024:(gi % 4 + 1) * 1024],
                             ALU.mult, inc="sgm")
                        sgm_need = c.cnt["sgm"]

                        def do_adds():
                            oadds = []
                            for half in range(2):
                                c.wait("vector", "sgm", sgm_need)
                                xfs.wait_item("vector", ("xfl", gi))
                                osem = f"ost{half}"
                                c.wait("vector", osem, c.cnt[osem])
                                c.do("vector", "tensor_tensor",
                                     outbuf[:, half * 512:(half + 1) * 512],
                                     sgbuf[:, half * 512:(half + 1) * 512],
                                     xfbuf[:, par * 1024 + half * 512:
                                           par * 1024 + (half + 1) * 512],
                                     ALU.add, inc="oadd")
                                oadds.append(c.cnt["oadd"])

                            def do_stores():
                                for half in range(2):
                                    osem = f"ost{half}"
                                    c.wait("scalar", "oadd", oadds[half])
                                    c.dma("scalar",
                                          out_hbm[b, tbg * 128:(tbg + 1) * 128,
                                                  half * 512:(half + 1) * 512],
                                          outbuf[:, half * 512:(half + 1) * 512],
                                          osem)
                            finals.insert(0, do_stores)
                        finals.insert(0, do_adds)
                    finals.append(do_loads)

                def drain_final(n=1):
                    for _ in range(n):
                        if finals:
                            finals.pop(0)()

                for j in range(NTSB):
                    for hp in range(12):
                        gidx = j * 12 + hp
                        buf = gidx % 2
                        for hh in range(2):
                            ht = hp * 2 + hh
                            c.wait("tensor", "sG", c.cnt["sG"] - 1)
                            c.wait("tensor", "hact", c.cnt["hact"] - 1)
                            if j == 0 and ht == 0:
                                c.wait("tensor", "E", c.at(("attn", b), "E"))
                            wgs.wait_item("tensor", ("wg", b, gidx))
                            wus.wait_item("tensor", ("wu", b, gidx))
                            hb = (ht % 2) * 512
                            for dt in range(NDT):
                                c.do("tensor", "matmul", psG[:, hb: hb + 512],
                                     wch[:, buf * 2048 + dt * 256 + hh * 128:
                                         buf * 2048 + dt * 256 + (hh + 1) * 128],
                                     xnT_sb[:, dt * 2048 + j * 512: dt * 2048 + (j + 1) * 512],
                                     start=(dt == 0), stop=(dt == NDT - 1),
                                     inc="G" if dt == NDT - 1 else None)
                            if ht == 0:
                                c.wait("tensor", "gev", ybank_guard["psU"])
                            for dt in range(NDT):
                                c.do("tensor", "matmul", psU[:, hb: hb + 512],
                                     wch[:, (2 + buf) * 2048 + dt * 256 + hh * 128:
                                         (2 + buf) * 2048 + dt * 256 + (hh + 1) * 128],
                                     xnT_sb[:, dt * 2048 + j * 512: dt * 2048 + (j + 1) * 512],
                                     start=(dt == 0), stop=(dt == NDT - 1),
                                     inc="U" if dt == NDT - 1 else None)
                            if j == 0 and hp == 0 and hh == 0:
                                c.do("scalar", "activation", lnt[:, 0:1],
                                     identt[:, 0:1], AF.Silu)
                            c.wait("scalar", "G", c.cnt["G"])
                            c.wait("scalar", "hact", c.cnt["hact"])
                            c.do("scalar", "activation",
                                 sGb[:, 0:512],
                                 psG[:, hb: hb + 512], AF.Silu, scale=SCALE_G, inc="sG")
                            c.wait("vector", "U", c.cnt["U"])
                            c.wait("vector", "sG", c.cnt["sG"])
                            if b > 0 or j > 0:
                                key = ("Ytot", b, j - 1) if j > 0 else ("Ytot", b - 1, NTSB - 1)
                                c.wait("vector", "Y", c.at(key, "Y"))
                            if b == 0 and j == 0 and ht == 0:
                                c.wait("vector", "cvt", c.at("conv_all", "cvt"))
                                c.wait("vector", "cvtw", NHT)
                            c.do("vector", "scalar_tensor_tensor",
                                 hact[:, ht * 512:(ht + 1) * 512],
                                 psU[:, hb: hb + 512], 1.0,
                                 sGb[:, 0:512],
                                 ALU.mult, ALU.mult, inc="hact")
                            c.put(("hactd", b, j, ht))
                        c.put(("wchG", b, gidx))
                        wgs.consumed(buf, "G", c.at(("wchG", b, gidx), "G"))
                        wus.consumed(buf, "U", c.at(("wchG", b, gidx), "U"))
                        load_chunk(b, gidx + 2)
                        drain_final(2)
                    for tbp in range(2):
                        for dh in range(2):
                            borrow = (dh == 1 and (j != NTSB - 1 or not epi))
                            ybank = psU if borrow else psY
                            guard_key = "psU" if borrow else "psO"
                            c.wait("tensor", "gev", ybank_guard[guard_key])
                            if j == 0 and tbp == 0 and dh == 0:
                                c.wait("tensor", "sevc", c.at(("attn", b), "sevc"))
                                if b == 0:
                                    c.wait("tensor", "cvtw", NHT)
                            if borrow and tbp == 0:
                                c.wait("tensor", "hact",
                                       c.at(("hactd", b, j, NHT - 1), "hact"))
                            for ht in range(NHT):
                                if tbp == 0 and dh == 0:
                                    c.wait("tensor", "hact",
                                           c.at(("hactd", b, j, ht), "hact"))
                                for tb in range(2):
                                    c.do("tensor", "matmul",
                                         ybank[:, tb * 512:(tb + 1) * 512],
                                         hact[:, ht * 512 + (tbp * 2 + tb) * 128:
                                              ht * 512 + (tbp * 2 + tb + 1) * 128],
                                         wdT_sb[:, ht * 1024 + dh * 512:
                                                ht * 1024 + (dh + 1) * 512],
                                         start=(ht == 0), stop=(ht == NHT - 1),
                                         inc="Y" if (ht == NHT - 1 and tb == 1) else None)
                            if j == NTSB - 1 and epi:
                                for _ in range(4):
                                    if epi:
                                        epi.pop(0)()
                            c.wait("scalar", "Y", c.cnt["Y"])
                            for tb in range(2):
                                tbg = j * 4 + tbp * 2 + tb
                                gi = b * NTB + tbg
                                if dh == 0 and gi >= 4:
                                    c.wait("scalar", "sgm", gi - 3)
                                c.do("scalar", "activation",
                                     gbuf[:, (gi % 4) * 1024 + dh * 512:
                                          (gi % 4) * 1024 + (dh + 1) * 512],
                                     ybank[:, tb * 512:(tb + 1) * 512], AF.Silu,
                                     scale=SCALE_Y, inc="gev")
                            ybank_guard[guard_key] = c.cnt["gev"]
                            if j == NTSB - 1 and tbp == 1 and dh == 1:
                                while epi:
                                    epi.pop(0)()
                            if b == NB - 1 and j == NTSB - 1:
                                drain_final(3)
                            for _ in range(3):
                                if bg:
                                    bg.pop(0)()
                        c.put(("gevp", b, j, tbp))
                        for tb in range(2):
                            queue_final(j, tbp, tb)
                    c.put(("Ytot", b, j))
                c.put(("ffn", b))
                return finals

            ln_phase(0)
            conversions_in()
            attn_phase(0)
            lo1 = ffn_phase(0, bg_steps=ln_phase(1, as_steps=True),
                            epi_tiles=[
                                (lambda tb=tb: (
                                    xnT_tile(1, tb),
                                    S0_chunk(1, tb // 4) if tb % 4 == 3 else None))
                                for tb in range(NTB)]
                            + [lambda: softmax_q(1, 0, embed_tr=False)])
            attn_phase(1, leftover=lo1)
            rest = ffn_phase(1)
            while rest:
                rest.pop(0)()

            if "s" in dbg:
                for b in range(NB):
                    for qq in range(NTB):
                        spl.wait_item("gpsimd", ("spl", b, qq))
                c.dma("gpsimd", dbg["s"][:, :, :], s_scr[:, :, :], "dbgs")
            if "xn" in dbg:
                for tb in range(NTB):
                    c.dma("gpsimd", dbg["xn"][NB - 1, tb * 128:(tb + 1) * 128, :],
                          xn_sb[:, tb * 1024:(tb + 1) * 1024], "dbgs")
            if "xnT" in dbg:
                for dt in range(NDT):
                    c.dma("gpsimd", dbg["xnT"][NB - 1, dt * 128:(dt + 1) * 128, :],
                          xnT_sb[:, dt * 2048:(dt + 1) * 2048], "dbgs")

            c.wait("scalar", "ost0", c.cnt["ost0"])
            c.wait("scalar", "ost1", c.cnt["ost1"])
            if dbg:
                c.wait("gpsimd", "dbgs", c.cnt["dbgs"])

        base = {"tensor": None, "vector": None, "scalar": None,
                "sync": None, "gpsimd": None}

        @block.tensor
        def _(tensor):
            walk("tensor", {**base, "tensor": tensor})

        @block.vector
        def _(vector):
            walk("vector", {**base, "vector": vector})

        @block.scalar
        def _(scalar):
            walk("scalar", {**base, "scalar": scalar})

        @block.sync
        def _(sync):
            walk("sync", {**base, "sync": sync})

        @block.gpsimd
        def _(gpsimd):
            walk("gpsimd", {**base, "gpsimd": gpsimd})

    return nc


def kernel(L0_attn_x, L0_ffn_w_gate, L0_ffn_w_up, L0_ffn_w_down):
    x = np.ascontiguousarray(np.asarray(L0_attn_x, dtype=np.float32))
    wgT = np.ascontiguousarray(np.asarray(L0_ffn_w_gate, dtype=np.int32).T)
    wuT = np.ascontiguousarray(np.asarray(L0_ffn_w_up, dtype=np.int32).T)
    wdT = np.ascontiguousarray(np.asarray(L0_ffn_w_down, dtype=np.int32).T)
    ident = np.eye(128, dtype=ml_dtypes.bfloat16)

    nc = build_program()
    in_maps = []
    for core in range(8):
        in_maps.append({
            "x": x[core * NB:(core + 1) * NB],
            "wgT": wgT, "wuT": wuT, "wdT": wdT,
            "ident": ident,
        })
    res = run_bass_kernel_spmd(nc, in_maps, core_ids=list(range(8)))
    out = np.concatenate([res.results[i]["out"] for i in range(8)], axis=0)
    if DEBUG_OUTPUTS:
        kernel.debug = [{k: np.asarray(res.results[i]["dbg_" + k])
                         for k in DEBUG_OUTPUTS} for i in range(8)]
    return out


# version 39
# speedup vs baseline: 1.1823x; 1.0092x over previous
import contextlib

import numpy as np
import ml_dtypes
import concourse.bass as bass
import concourse.mybir as mybir
from concourse.bass_utils import run_bass_kernel_spmd

F32 = mybir.dt.float32
BF16 = mybir.dt.bfloat16
I32 = mybir.dt.int32
AF = mybir.ActivationFunctionType
ALU = mybir.AluOpType
AX = mybir.AxisListType

NB = 2
T = 2048
D = 1024
H = 3072
NTB = T // 128
NDT = D // 128
NHT = H // 128
TSB = 512
NTSB = T // TSB
EPS = 1e-5
INV127 = float(np.float32(1.0) / np.float32(127.0))
SCALE_G = INV127
SCALE_Y = INV127 * INV127

DEBUG_OUTPUTS = []


class Ctx:

    def __init__(self, live, proxies, sems, stash):
        self.live = live
        self.prox = proxies
        self.sems = sems
        self.cnt = {k: 0 for k in sems}
        self.stash = stash

    def wait(self, eng, sem, val):
        if val is None or val <= 0:
            return
        if self.live == eng:
            self.prox[eng].wait_ge(self.sems[sem], val)

    def do(self, eng, method, *a, inc=None, incval=1, **kw):
        if self.live == eng:
            ins = getattr(self.prox[eng], method)(*a, **kw)
            if inc is not None:
                ins.then_inc(self.sems[inc], incval)
        if inc is not None:
            self.cnt[inc] += incval

    def dma(self, eng, out, in_, sem, **kw):
        self.do(eng, "dma_start", inc=sem, incval=16, out=out, in_=in_, **kw)

    def put(self, key):
        if key not in self.stash:
            self.stash[key] = dict(self.cnt)

    def at(self, key, sem):
        return self.stash[key][sem]


class DStream:

    def __init__(self, c: Ctx, name, n):
        self.c, self.name, self.n = c, name, n
        self.guard = [None] * n
        self.items = {}
        self.i = 0

    @staticmethod
    def sem_names(name, n):
        return [f"{name}{i}" for i in range(n)]

    def load(self, eng, key, out_of_slot, in_ap, **kw):
        s = self.i % self.n
        self.i += 1
        g = self.guard[s]
        if g:
            self.c.wait(eng, g[0], g[1])
        sem = f"{self.name}{s}"
        self.c.dma(eng, out_of_slot(s), in_ap, sem, **kw)
        self.items[key] = (sem, self.c.cnt[sem])
        return s

    def consumed(self, slot, sem, val):
        self.guard[slot] = (sem, val)

    def wait_item(self, eng, key):
        sem, v = self.items[key]
        self.c.wait(eng, sem, v)

    def totals(self):
        out = {}
        for s in range(self.n):
            out[f"{self.name}{s}"] = self.c.cnt[f"{self.name}{s}"]
        return out


def build_program():
    nc = bass.Bass("TRN2", target_bir_lowering=False)

    x_hbm = nc.dram_tensor("x", [NB, T, D], F32, kind="ExternalInput").ap()
    wgT_hbm = nc.dram_tensor("wgT", [D, H], I32, kind="ExternalInput").ap()
    wuT_hbm = nc.dram_tensor("wuT", [D, H], I32, kind="ExternalInput").ap()
    wdT_hbm = nc.dram_tensor("wdT", [H, D], I32, kind="ExternalInput").ap()
    ident_hbm = nc.dram_tensor("ident", [128, 128], BF16, kind="ExternalInput").ap()
    out_hbm = nc.dram_tensor("out", [NB, T, D], F32, kind="ExternalOutput").ap()

    wg_scr = nc.dram_tensor("wg_scr", [D, H], BF16).ap()
    wu_scr = nc.dram_tensor("wu_scr", [D, H], BF16).ap()
    s_scr = nc.dram_tensor("s_scr", [NB, T, D], BF16).ap()

    dbg = {}
    for name in DEBUG_OUTPUTS:
        shp = {"xn": [NB, T, D], "xnT": [NB, D, T], "s": [NB, T, D]}[name]
        dt = {"xn": F32, "xnT": F32, "s": BF16}[name]
        dbg[name] = nc.dram_tensor("dbg_" + name, shp, dt, kind="ExternalOutput").ap()

    sem_names = (
        ["idld", "cvt", "cvtw", "bn", "rstd", "xnw",
         "Ttr", "Tev", "S", "mx", "E", "rec", "O", "sevc",
         "G", "U", "sG", "hact", "Y", "gev", "sgm", "oadd", "dbgs"]
        + DStream.sem_names("xld", 3)
        + DStream.sem_names("cvc", 2)
        + DStream.sem_names("cwc", 2)
        + DStream.sem_names("cvo", 2)
        + DStream.sem_names("spl", 2)
        + DStream.sem_names("srl", 2)
        + DStream.sem_names("xfl", 2)
        + DStream.sem_names("wgl", 2)
        + DStream.sem_names("wul", 2)
        + DStream.sem_names("ost", 2)
    )

    shared_stash = {}

    with contextlib.ExitStack() as stack:
        ent = stack.enter_context
        wdT_sb = ent(nc.sbuf_tensor("wdT_sb", [128, NHT * 1024], BF16))
        xn_sb = ent(nc.sbuf_tensor("xn_sb", [128, NTB * 1024], BF16))
        xnT_sb = ent(nc.sbuf_tensor("xnT_sb", [128, NDT * 2048], BF16))
        hact = ent(nc.sbuf_tensor("hact", [128, NHT * 512], BF16))
        Ebuf = ent(nc.sbuf_tensor("Ebuf", [128, 2 * 2048], BF16))
        aT = ent(nc.sbuf_tensor("aT", [128, 2048], BF16))
        wch = ent(nc.sbuf_tensor("wch", [128, 4 * 2048], BF16))
        xbuf = ent(nc.sbuf_tensor("xbuf", [128, 3 * 1024], F32))
        xfbuf = ent(nc.sbuf_tensor("xfbuf", [128, 2 * 1024], F32))
        outbuf = ent(nc.sbuf_tensor("outbuf", [128, 2 * 512], F32))
        sqb = ent(nc.sbuf_tensor("sqb", [128, 2 * 1024], BF16))
        srlb = ent(nc.sbuf_tensor("srlb", [128, 2 * 1024], BF16))
        gbuf = ent(nc.sbuf_tensor("gbuf", [128, 4 * 1024], BF16))
        sgbuf = ent(nc.sbuf_tensor("sgbuf", [128, 1024], BF16))
        sGb = ent(nc.sbuf_tensor("sGb", [128, 512], BF16))
        identt = ent(nc.sbuf_tensor("identt", [128, 128], BF16))
        mvAll = ent(nc.sbuf_tensor("mvAll", [128, 32], F32))
        bnst = ent(nc.sbuf_tensor("bnst", [128, 12], F32))
        veps = ent(nc.sbuf_tensor("veps", [128, 16], F32))
        lnt = ent(nc.sbuf_tensor("lnt", [128, 16], F32))
        rstd16 = ent(nc.sbuf_tensor("rstd16", [128, 16], F32))
        nmr16 = ent(nc.sbuf_tensor("nmr16", [128, 16], F32))
        nmx = ent(nc.sbuf_tensor("nmx", [128, 2], F32))
        sume = ent(nc.sbuf_tensor("sume", [128, 2], F32))
        recq = ent(nc.sbuf_tensor("recq", [128, 2], F32))
        psT = ent(nc.psum_tensor("psT", [128, 2048], BF16))
        psF = ent(nc.psum_tensor("psF", [128, 3072], F32))
        sems = {n: ent(nc.semaphore(n)) for n in sem_names}
        block = ent(nc.Block())

        def walk(live, proxies):
            c = Ctx(live, proxies, sems, shared_stash)
            mv2 = mvAll[:].rearrange("p (t two) -> p t two", two=2)
            mv_mean = mv2[:, :, 0]
            mv_var = mv2[:, :, 1]

            psS = psF[:, 0:2048]
            psG = psF[:, 0:1024]
            psU = psF[:, 1024:2048]
            psO = psF[:, 2048:3072]
            psY = psO

            xs = DStream(c, "xld", 3)
            spl = DStream(c, "spl", 2)
            srs = DStream(c, "srl", 2)
            xfs = DStream(c, "xfl", 2)
            wgs = DStream(c, "wgl", 2)
            wus = DStream(c, "wul", 2)
            obs = DStream(c, "ost", 2)
            Tstate = {"g": 0}
            Ehist = []

            c.dma("sync", identt[:], ident_hbm, "idld")

            stg_i32 = [hact[:, 0:6144].bitcast(I32), hact[:, 6144:12288].bitcast(I32)]
            cw_bf = [wch[:, 0:3072], wch[:, 3072:6144]]
            N_CONV = 16
            conv_state = {"loads": [], "done": 0, "out": 0}
            wd_state = {"done": 0, "loaded": 0, "lsem": []}

            def conversions_in():
                for k in range(16):
                    par = k % 2
                    if k >= 2:
                        c.wait("gpsimd", "cvt", k - 1)
                    src = (wgT_hbm, wuT_hbm)[k // 8]
                    ch = k % 8
                    c.dma("gpsimd", stg_i32[par],
                          src[ch * 128:(ch + 1) * 128, :], f"cvc{par}")
                    conv_state["loads"].append((f"cvc{par}", c.cnt[f"cvc{par}"]))

            def conv_cvt_step(n):
                for _ in range(n):
                    k = conv_state["done"]
                    if k >= N_CONV:
                        return
                    if 2 <= k < 16 and conv_state["out"] < k - 1:
                        return
                    conv_state["done"] += 1
                    par = k % 2
                    sem, v = conv_state["loads"][k]
                    c.wait("vector", sem, v)
                    if k >= 2:
                        c.wait("vector", f"cvo{par}", 16 * ((k - par) // 2))
                    c.do("vector", "tensor_copy", cw_bf[par], stg_i32[par],
                         inc="cvt")
                    if conv_state["done"] == N_CONV:
                        c.put("conv_all")

            stg_wd = [srlb[:, 0:2048].bitcast(I32), gbuf[:, 0:2048].bitcast(I32)]

            def conv_wd_step(n):
                for _ in range(n):
                    ht = wd_state["done"]
                    if ht >= NHT:
                        return
                    wd_state["done"] += 1
                    while wd_state["loaded"] <= min(ht + 1, NHT - 1):
                        lt = wd_state["loaded"]
                        wd_state["loaded"] += 1
                        sl = lt % 2
                        if lt >= 2:
                            c.wait("gpsimd", "cvtw", lt - 1)
                        c.dma("gpsimd", stg_wd[sl],
                              wdT_hbm[lt * 128:(lt + 1) * 128, :], f"cwc{sl}")
                        wd_state["lsem"].append((f"cwc{sl}", c.cnt[f"cwc{sl}"]))
                    sem, v = wd_state["lsem"][ht]
                    c.wait("gpsimd", sem, v)
                    c.do("gpsimd", "tensor_copy",
                         wdT_sb[:, ht * 1024:(ht + 1) * 1024],
                         stg_wd[ht % 2], inc="cvtw")

            def conv_out_step():
                while conv_state["out"] < min(conv_state["done"], 16):
                    k = conv_state["out"]
                    conv_state["out"] += 1
                    par = k % 2
                    c.wait("scalar", "cvt", k + 1)
                    dst = (wg_scr, wu_scr)[k // 8]
                    ch = k % 8
                    c.dma("scalar", dst[ch * 128:(ch + 1) * 128, :], cw_bf[par],
                          f"cvo{par}")
                    if conv_state["out"] == 16:
                        c.put("conv_done")

            def conv_wait(eng):
                c.wait(eng, "cvo0", c.at("conv_done", "cvo0"))
                c.wait(eng, "cvo1", c.at("conv_done", "cvo1"))

            def ln_phase(b, as_steps=False):
                steps = []

                def step(fn):
                    if as_steps:
                        steps.append(fn)
                    else:
                        fn()

                def x_eng(tb):
                    return "scalar" if (b == 0 and tb % 2 == 1) else "sync"

                def p1_tile(tb):
                    par = xs.load(x_eng(tb), ("p1", b, tb),
                                  lambda s: xbuf[:, s * 1024:(s + 1) * 1024],
                                  x_hbm[b, tb * 128:(tb + 1) * 128, :])
                    xs.wait_item("vector", ("p1", b, tb))
                    c.wait("vector", "bn", c.cnt["bn"])
                    c.do("vector", "bn_stats", bnst[:, 0:6],
                         xbuf[:, par * 1024: par * 1024 + 512], inc="bn")
                    c.wait("vector", "bn", c.cnt["bn"])
                    c.do("vector", "bn_stats", bnst[:, 6:12],
                         xbuf[:, par * 1024 + 512: par * 1024 + 1024], inc="bn")
                    c.wait("vector", "bn", c.cnt["bn"])
                    c.do("vector", "bn_aggr", mvAll[:, 2 * tb: 2 * tb + 2],
                         bnst[:, 0:12], inc="bn")
                    xs.consumed(par, "bn", c.cnt["bn"])

                nmr_holder = [None]

                def rstd_block():
                    c.wait("vector", "bn", c.cnt["bn"])
                    c.do("vector", "tensor_scalar_add", veps[:, 0:16], mv_var,
                         EPS, inc="bn")
                    c.wait("scalar", "bn", c.cnt["bn"])
                    c.do("scalar", "activation", lnt[:, 0:16], veps[:, 0:16],
                         AF.Ln, inc="rstd")
                    c.wait("scalar", "rstd", c.cnt["rstd"])
                    c.do("scalar", "activation", rstd16[:, 0:16], lnt[:, 0:16],
                         AF.Exp, scale=-0.5, inc="rstd")
                    c.wait("vector", "rstd", c.cnt["rstd"])
                    c.do("vector", "scalar_tensor_tensor", nmr16[:, 0:16],
                         mv_mean, -1.0, rstd16[:, 0:16], ALU.mult, ALU.mult,
                         inc="bn")
                    nmr_holder[0] = c.cnt["bn"]

                def p2_tile(tb):
                    eng = "gpsimd"
                    par = xs.load(x_eng(tb), ("p2", b, tb),
                                  lambda s: xbuf[:, s * 1024:(s + 1) * 1024],
                                  x_hbm[b, tb * 128:(tb + 1) * 128, :])
                    xs.wait_item(eng, ("p2", b, tb))
                    c.wait(eng, "bn", nmr_holder[0])
                    if b >= 1 and tb == 0:
                        c.wait(eng, "O", c.at(("attn", b - 1), "O"))
                    c.do(eng, "tensor_scalar",
                         xn_sb[:, tb * 1024:(tb + 1) * 1024],
                         xbuf[:, par * 1024:(par + 1) * 1024],
                         rstd16[:, tb:tb + 1], nmr16[:, tb:tb + 1],
                         ALU.mult, ALU.add, inc="xnw")
                    xs.consumed(par, "xnw", c.cnt["xnw"])
                    if b == 0:
                        xnT_tile(b, tb)
                        if tb % 4 == 3:
                            S0_chunk(b, tb // 4)
                    if tb == NTB - 1:
                        c.put(("xnw", b))

                def rstd_half(lo, hi):
                    c.wait("vector", "bn", c.cnt["bn"])
                    c.do("vector", "tensor_scalar_add", veps[:, lo:hi],
                         mv_var[:, lo:hi], EPS, inc="bn")
                    c.wait("scalar", "bn", c.cnt["bn"])
                    c.do("scalar", "activation", lnt[:, lo:hi], veps[:, lo:hi],
                         AF.Ln, inc="rstd")
                    c.wait("scalar", "rstd", c.cnt["rstd"])
                    c.do("scalar", "activation", rstd16[:, lo:hi], lnt[:, lo:hi],
                         AF.Exp, scale=-0.5, inc="rstd")
                    c.wait("vector", "rstd", c.cnt["rstd"])
                    c.do("vector", "scalar_tensor_tensor", nmr16[:, lo:hi],
                         mv_mean[:, lo:hi], -1.0, rstd16[:, lo:hi],
                         ALU.mult, ALU.mult, inc="bn")
                    nmr_holder[0] = c.cnt["bn"]

                if b == 0 and not as_steps:
                    for tb in range(2):
                        p1_tile(tb)
                    rstd_half(0, 2)
                    for pr in range(1, 8):
                        for tb in range(2):
                            p1_tile(2 * pr + tb)
                            p2_tile(2 * (pr - 1) + tb)
                        rstd_half(2 * pr, 2 * pr + 2)
                    for tb in range(14, NTB):
                        p2_tile(tb)
                    softmax_q(0, 0, embed_tr=False)
                    return None
                for tb in range(NTB):
                    step(lambda tb=tb: p1_tile(tb))
                step(rstd_block)
                for tb in range(NTB):
                    step(lambda tb=tb: p2_tile(tb))
                if as_steps:
                    return steps

            def xnT_tile(b, tb):
                if tb == 0:
                    c.put(("xnTbase", b))
                grp = Tstate["g"]
                bank = (grp % 2) * 1024
                c.wait("tensor", "Tev", grp - 1)
                c.wait("tensor", "xnw", b * NTB + tb + 1)
                if grp == 0:
                    c.wait("tensor", "idld", 16)
                for dt in range(NDT):
                    c.do("tensor", "transpose",
                         psT[:, bank + dt * 128: bank + (dt + 1) * 128],
                         xn_sb[:, tb * 1024 + dt * 128: tb * 1024 + (dt + 1) * 128],
                         identt[:], inc="Ttr" if dt == NDT - 1 else None)
                c.wait("vector", "Ttr", c.cnt["Ttr"])
                if b >= 1 and tb == 0:
                    c.wait("vector", "G", c.cnt["G"])
                    c.wait("vector", "U", c.cnt["U"])
                o = xnT_sb[:].bitcast(I32).rearrange(
                    "p (dt t) -> p dt t", dt=NDT)[:, :, tb * 64:(tb + 1) * 64]
                i_ = psT[:, bank: bank + NDT * 128].bitcast(I32).rearrange(
                    "p (dt t) -> p dt t", dt=NDT)
                c.do("vector", "tensor_copy", o, i_, inc="Tev")
                Tstate["g"] += 1
                if tb == NTB - 1:
                    c.put(("xnT", b))

            def S0_chunk(b, kc):
                if kc == 0:
                    c.wait("tensor", "E", c.cnt["E"])
                    if b > 0:
                        c.wait("tensor", "sG", c.cnt["sG"])
                        c.wait("tensor", "hact", c.cnt["hact"])
                base = c.at(("xnTbase", b), "Tev")
                c.wait("tensor", "Tev", base + 4 * (kc + 1))
                for dt in range(NDT):
                    c.do("tensor", "matmul",
                         psS[:, kc * 512:(kc + 1) * 512],
                         xnT_sb[:, dt * 2048: dt * 2048 + 128],
                         xnT_sb[:, dt * 2048 + kc * 512: dt * 2048 + (kc + 1) * 512],
                         start=(dt == 0), stop=(dt == NDT - 1),
                         inc="S" if (dt == NDT - 1 and kc == 3) else None)

            def tr_group(b, qq, g2):
                grp = Tstate["g"]
                bank = (grp % 2) * 1024
                e0 = (qq % 2) * 2048
                c.wait("tensor", "Tev", grp - 1)
                if g2 == 0:
                    c.wait("tensor", "E", c.at(("E", b, qq), "E"))
                for i in range(8):
                    kt = g2 * 8 + i
                    c.do("tensor", "transpose",
                         psT[:, bank + i * 128: bank + (i + 1) * 128],
                         Ebuf[:, e0 + kt * 128: e0 + (kt + 1) * 128],
                         identt[:], inc="Ttr" if i == 7 else None)
                if g2 == 1:
                    c.put(("Etr", b, qq))
                    Ehist.append(("Etr", b, qq))
                return bank

            def tr_evac(b, qq, g2, bank):
                c.wait("vector", "Ttr", c.cnt["Ttr"])
                c.do("vector", "tensor_copy",
                     aT[:, g2 * 1024:(g2 + 1) * 1024].bitcast(I32),
                     psT[:, bank: bank + 1024].bitcast(I32), inc="Tev")
                c.put(("aTev", b, qq, g2))
                Tstate["g"] += 1

            def softmax_q(b, q, embed_tr):
                if q == 0 and b > 0:
                    c.do("scalar", "activation", lnt[:, 0:1], identt[:, 0:1],
                         AF.Exp)
                c.wait("vector", "S", c.cnt["S"])
                if q >= 2:
                    c.wait("vector", "sevc", c.cnt["sevc"])
                c.do("vector", "tensor_reduce", nmx[:, q % 2: q % 2 + 1],
                     psS[:, 0:2048], AX.X, ALU.max, negate=True, inc="mx")
                if embed_tr:
                    bankB = tr_group(b, q - 1, 1)
                    tr_evac(b, q - 1, 1, bankB)
                c.wait("scalar", "mx", c.cnt["mx"])
                prev_same = [k for k in Ehist if k[2] % 2 == q % 2]
                if prev_same:
                    c.wait("scalar", "Ttr", c.at(prev_same[-1], "Ttr"))
                if q >= 2:
                    c.wait("scalar", "rec", c.cnt["rec"] - 1)
                c.do("scalar", "activation",
                     Ebuf[:, (q % 2) * 2048:(q % 2 + 1) * 2048],
                     psS[:, 0:2048], AF.Exp,
                     bias=nmx[:, q % 2: q % 2 + 1],
                     accum_out=sume[:, q % 2: q % 2 + 1], inc="E")
                c.wait("scalar", "E", c.cnt["E"])
                c.do("scalar", "copy", bnst[:, q % 2: q % 2 + 1],
                     sume[:, q % 2: q % 2 + 1], inc="E")
                c.wait("vector", "E", c.cnt["E"])
                c.do("vector", "reciprocal", recq[:, q % 2: q % 2 + 1],
                     bnst[:, q % 2: q % 2 + 1], inc="rec")
                c.put(("E", b, q))
                c.put(("rec", b, q))

            def attn_phase(b, leftover=None):
                lo = leftover if leftover is not None else []
                assert ("xnT", b) in c.stash

                for q in range(NTB + 1):
                    qq = q - 1
                    conv_out_step()
                    conv_wd_step(2)
                    if q == 0 or q == NTB:
                        conv_cvt_step(3)
                    if q < NTB:
                        if q >= 1:
                            c.wait("tensor", "E", c.cnt["E"])
                            for dt in range(NDT):
                                for kc in range(4):
                                    c.do("tensor", "matmul",
                                         psS[:, kc * 512:(kc + 1) * 512],
                                         xnT_sb[:, dt * 2048 + q * 128: dt * 2048 + (q + 1) * 128],
                                         xnT_sb[:, dt * 2048 + kc * 512: dt * 2048 + (kc + 1) * 512],
                                         start=(dt == 0), stop=(dt == NDT - 1),
                                         inc="S" if (dt == NDT - 1 and kc == 3) else None)
                    if q >= 1:
                        bankA = tr_group(b, qq, 0)
                        tr_evac(b, qq, 0, bankA)
                    if q < NTB and q >= 1:
                        softmax_q(b, q, embed_tr=True)
                        conv_cvt_step(3)
                        for _ in range(2):
                            if lo:
                                lo.pop(0)()
                    if q >= 1:
                        if q == NTB:
                            bankB = tr_group(b, qq, 1)
                            tr_evac(b, qq, 1, bankB)
                        c.wait("tensor", "sevc", c.cnt["sevc"])
                        if b > 0 and qq == 0:
                            c.wait("tensor", "gev", c.at(("ffn", b - 1), "gev"))
                        for g2 in range(2):
                            c.wait("tensor", "Tev", c.at(("aTev", b, qq, g2), "Tev"))
                            for i in range(8):
                                kt = g2 * 8 + i
                                for dh in range(2):
                                    c.do("tensor", "matmul",
                                         psO[:, dh * 512:(dh + 1) * 512],
                                         aT[:, kt * 128:(kt + 1) * 128],
                                         xn_sb[:, kt * 1024 + dh * 512:
                                               kt * 1024 + (dh + 1) * 512],
                                         start=(kt == 0), stop=(kt == 15),
                                         inc="O" if (kt == 15 and dh == 1) else None)
                        c.wait("scalar", "O", c.cnt["O"])
                        c.wait("scalar", "rec", c.at(("rec", b, qq), "rec"))
                        spq = ("spl", b, qq - 2)
                        if spq in spl.items:
                            sem, v = spl.items[spq]
                            c.wait("scalar", sem, v)
                        elif qq < 2 and b > 0:
                            sem, v = spl.items[("spl", b - 1, qq - 2 + NTB)]
                            c.wait("scalar", sem, v)
                        c.do("scalar", "activation",
                             sqb[:, (qq % 2) * 1024:(qq % 2 + 1) * 1024],
                             psO[:, 0:1024], AF.Copy,
                             scale=recq[:, qq % 2: qq % 2 + 1], inc="sevc")
                        c.wait("scalar", "sevc", c.cnt["sevc"])
                        spl.load("scalar", ("spl", b, qq),
                                 lambda s: s_scr[b, qq * 128:(qq + 1) * 128, :],
                                 sqb[:, (qq % 2) * 1024:(qq % 2 + 1) * 1024])
                        if qq == 12:
                            load_chunk(b, 0)
                            load_chunk(b, 1)
                while conv_state["done"] < N_CONV or conv_state["out"] < 16:
                    conv_cvt_step(1)
                    conv_out_step()
                conv_wd_step(NHT)
                c.put(("attn", b))

            wg3 = wg_scr.rearrange("(dt p) hh -> p dt hh", p=128)
            wu3 = wu_scr.rearrange("(dt p) hh -> p dt hh", p=128)

            def load_chunk(b, lidx):
                if lidx >= NTSB * 12:
                    return
                hp = lidx % 12
                if b == 0 and lidx == 0:
                    conv_wait("scalar")
                    conv_wait("sync")
                wgs.load("scalar", ("wg", b, lidx),
                         lambda s: wch[:, s * 2048:(s + 1) * 2048].rearrange(
                             "p (dt h) -> p dt h", h=256),
                         wg3[:, :, hp * 256:(hp + 1) * 256])
                wus.load("sync", ("wu", b, lidx),
                         lambda s: wch[:, (2 + s) * 2048:(3 + s) * 2048].rearrange(
                             "p (dt h) -> p dt h", h=256),
                         wu3[:, :, hp * 256:(hp + 1) * 256])

            def ffn_phase(b, bg_steps=None, epi_tiles=None):
                bg = list(bg_steps) if bg_steps else []
                epi = list(epi_tiles) if epi_tiles else []
                ybank_guard = {"psO": 0, "psU": 0}
                finals = []

                def queue_final(j, tbp, tb):
                    tbg = j * 4 + tbp * 2 + tb
                    gi = b * NTB + tbg
                    par = gi % 2
                    gev_need = c.at(("gevp", b, j, tbp), "gev")

                    def do_loads():
                        spl.wait_item("sync", ("spl", b, tbg))
                        sslot = srs.load("sync", ("srl", gi),
                                         lambda s: srlb[:, s * 1024:(s + 1) * 1024],
                                         s_scr[b, tbg * 128:(tbg + 1) * 128, :])
                        srs.consumed(sslot, "sgm", gi + 1)
                        xslot = xfs.load("sync", ("xfl", gi),
                                         lambda s: xfbuf[:, s * 1024:(s + 1) * 1024],
                                         x_hbm[b, tbg * 128:(tbg + 1) * 128, :])
                        xfs.consumed(xslot, "oadd", 2 * (gi + 1))
                        finals.insert(0, do_mul)

                    def do_mul():
                        meng = "vector" if gi >= 2 * NTB - 4 else "gpsimd"
                        c.wait(meng, "gev", gev_need)
                        srs.wait_item(meng, ("srl", gi))
                        if gi >= 1:
                            c.wait(meng, "oadd", 2 * gi)
                        c.do(meng, "tensor_tensor",
                             sgbuf[:, 0:1024],
                             srlb[:, par * 1024:(par + 1) * 1024],
                             gbuf[:, (gi % 4) * 1024:(gi % 4 + 1) * 1024],
                             ALU.mult, inc="sgm")
                        sgm_need = c.cnt["sgm"]

                        def do_adds():
                            oadds = []
                            for half in range(2):
                                c.wait("vector", "sgm", sgm_need)
                                xfs.wait_item("vector", ("xfl", gi))
                                osem = f"ost{half}"
                                c.wait("vector", osem, c.cnt[osem])
                                c.do("vector", "tensor_tensor",
                                     outbuf[:, half * 512:(half + 1) * 512],
                                     sgbuf[:, half * 512:(half + 1) * 512],
                                     xfbuf[:, par * 1024 + half * 512:
                                           par * 1024 + (half + 1) * 512],
                                     ALU.add, inc="oadd")
                                oadds.append(c.cnt["oadd"])

                            def do_stores():
                                for half in range(2):
                                    osem = f"ost{half}"
                                    c.wait("scalar", "oadd", oadds[half])
                                    c.dma("scalar",
                                          out_hbm[b, tbg * 128:(tbg + 1) * 128,
                                                  half * 512:(half + 1) * 512],
                                          outbuf[:, half * 512:(half + 1) * 512],
                                          osem)
                            finals.insert(0, do_stores)
                        finals.insert(0, do_adds)
                    finals.append(do_loads)

                def drain_final(n=1):
                    for _ in range(n):
                        if finals:
                            finals.pop(0)()

                for j in range(NTSB):
                    for hp in range(12):
                        gidx = j * 12 + hp
                        buf = gidx % 2
                        for hh in range(2):
                            ht = hp * 2 + hh
                            c.wait("tensor", "sG", c.cnt["sG"] - 1)
                            c.wait("tensor", "hact", c.cnt["hact"] - 1)
                            if j == 0 and ht == 0:
                                c.wait("tensor", "E", c.at(("attn", b), "E"))
                            wgs.wait_item("tensor", ("wg", b, gidx))
                            wus.wait_item("tensor", ("wu", b, gidx))
                            hb = (ht % 2) * 512
                            for dt in range(NDT):
                                c.do("tensor", "matmul", psG[:, hb: hb + 512],
                                     wch[:, buf * 2048 + dt * 256 + hh * 128:
                                         buf * 2048 + dt * 256 + (hh + 1) * 128],
                                     xnT_sb[:, dt * 2048 + j * 512: dt * 2048 + (j + 1) * 512],
                                     start=(dt == 0), stop=(dt == NDT - 1),
                                     inc="G" if dt == NDT - 1 else None)
                            if ht == 0:
                                c.wait("tensor", "gev", ybank_guard["psU"])
                            for dt in range(NDT):
                                c.do("tensor", "matmul", psU[:, hb: hb + 512],
                                     wch[:, (2 + buf) * 2048 + dt * 256 + hh * 128:
                                         (2 + buf) * 2048 + dt * 256 + (hh + 1) * 128],
                                     xnT_sb[:, dt * 2048 + j * 512: dt * 2048 + (j + 1) * 512],
                                     start=(dt == 0), stop=(dt == NDT - 1),
                                     inc="U" if dt == NDT - 1 else None)
                            if j == 0 and hp == 0 and hh == 0:
                                c.do("scalar", "activation", lnt[:, 0:1],
                                     identt[:, 0:1], AF.Silu)
                            c.wait("scalar", "G", c.cnt["G"])
                            c.wait("scalar", "hact", c.cnt["hact"])
                            c.do("scalar", "activation",
                                 sGb[:, 0:512],
                                 psG[:, hb: hb + 512], AF.Silu, scale=SCALE_G, inc="sG")
                            c.wait("vector", "U", c.cnt["U"])
                            c.wait("vector", "sG", c.cnt["sG"])
                            if b > 0 or j > 0:
                                key = ("Ytot", b, j - 1) if j > 0 else ("Ytot", b - 1, NTSB - 1)
                                c.wait("vector", "Y", c.at(key, "Y"))
                            if b == 0 and j == 0 and ht == 0:
                                c.wait("vector", "cvt", c.at("conv_all", "cvt"))
                                c.wait("vector", "cvtw", NHT)
                            c.do("vector", "scalar_tensor_tensor",
                                 hact[:, ht * 512:(ht + 1) * 512],
                                 psU[:, hb: hb + 512], 1.0,
                                 sGb[:, 0:512],
                                 ALU.mult, ALU.mult, inc="hact")
                            c.put(("hactd", b, j, ht))
                        c.put(("wchG", b, gidx))
                        wgs.consumed(buf, "G", c.at(("wchG", b, gidx), "G"))
                        wus.consumed(buf, "U", c.at(("wchG", b, gidx), "U"))
                        load_chunk(b, gidx + 2)
                        drain_final(2)
                    for tbp in range(2):
                        for dh in range(2):
                            borrow = (dh == 1 and (j != NTSB - 1 or not epi))
                            ybank = psU if borrow else psY
                            guard_key = "psU" if borrow else "psO"
                            c.wait("tensor", "gev", ybank_guard[guard_key])
                            if j == 0 and tbp == 0 and dh == 0:
                                c.wait("tensor", "sevc", c.at(("attn", b), "sevc"))
                                if b == 0:
                                    c.wait("tensor", "cvtw", NHT)
                            if borrow and tbp == 0:
                                c.wait("tensor", "hact",
                                       c.at(("hactd", b, j, NHT - 1), "hact"))
                            for ht in range(NHT):
                                if tbp == 0 and dh == 0:
                                    c.wait("tensor", "hact",
                                           c.at(("hactd", b, j, ht), "hact"))
                                for tb in range(2):
                                    c.do("tensor", "matmul",
                                         ybank[:, tb * 512:(tb + 1) * 512],
                                         hact[:, ht * 512 + (tbp * 2 + tb) * 128:
                                              ht * 512 + (tbp * 2 + tb + 1) * 128],
                                         wdT_sb[:, ht * 1024 + dh * 512:
                                                ht * 1024 + (dh + 1) * 512],
                                         start=(ht == 0), stop=(ht == NHT - 1),
                                         inc="Y" if (ht == NHT - 1 and tb == 1) else None)
                            if j == NTSB - 1 and epi:
                                for _ in range(4):
                                    if epi:
                                        epi.pop(0)()
                            c.wait("scalar", "Y", c.cnt["Y"])
                            for tb in range(2):
                                tbg = j * 4 + tbp * 2 + tb
                                gi = b * NTB + tbg
                                if dh == 0 and gi >= 4:
                                    c.wait("scalar", "sgm", gi - 3)
                                c.do("scalar", "activation",
                                     gbuf[:, (gi % 4) * 1024 + dh * 512:
                                          (gi % 4) * 1024 + (dh + 1) * 512],
                                     ybank[:, tb * 512:(tb + 1) * 512], AF.Silu,
                                     scale=SCALE_Y, inc="gev")
                            ybank_guard[guard_key] = c.cnt["gev"]
                            if j == NTSB - 1 and tbp == 1 and dh == 1:
                                while epi:
                                    epi.pop(0)()
                            if b == NB - 1 and j == NTSB - 1:
                                drain_final(3)
                            for _ in range(3):
                                if bg:
                                    bg.pop(0)()
                        c.put(("gevp", b, j, tbp))
                        for tb in range(2):
                            queue_final(j, tbp, tb)
                    c.put(("Ytot", b, j))
                c.put(("ffn", b))
                return finals

            ln_phase(0)
            conversions_in()
            attn_phase(0)
            lo1 = ffn_phase(0, bg_steps=ln_phase(1, as_steps=True),
                            epi_tiles=[
                                (lambda tb=tb: (
                                    xnT_tile(1, tb),
                                    S0_chunk(1, tb // 4) if tb % 4 == 3 else None))
                                for tb in range(NTB)]
                            + [lambda: softmax_q(1, 0, embed_tr=False)])
            attn_phase(1, leftover=lo1)
            rest = ffn_phase(1)
            while rest:
                rest.pop(0)()

            if "s" in dbg:
                for b in range(NB):
                    for qq in range(NTB):
                        spl.wait_item("gpsimd", ("spl", b, qq))
                c.dma("gpsimd", dbg["s"][:, :, :], s_scr[:, :, :], "dbgs")
            if "xn" in dbg:
                for tb in range(NTB):
                    c.dma("gpsimd", dbg["xn"][NB - 1, tb * 128:(tb + 1) * 128, :],
                          xn_sb[:, tb * 1024:(tb + 1) * 1024], "dbgs")
            if "xnT" in dbg:
                for dt in range(NDT):
                    c.dma("gpsimd", dbg["xnT"][NB - 1, dt * 128:(dt + 1) * 128, :],
                          xnT_sb[:, dt * 2048:(dt + 1) * 2048], "dbgs")

            c.wait("scalar", "ost0", c.cnt["ost0"])
            c.wait("scalar", "ost1", c.cnt["ost1"])
            if dbg:
                c.wait("gpsimd", "dbgs", c.cnt["dbgs"])

        base = {"tensor": None, "vector": None, "scalar": None,
                "sync": None, "gpsimd": None}

        @block.tensor
        def _(tensor):
            walk("tensor", {**base, "tensor": tensor})

        @block.vector
        def _(vector):
            walk("vector", {**base, "vector": vector})

        @block.scalar
        def _(scalar):
            walk("scalar", {**base, "scalar": scalar})

        @block.sync
        def _(sync):
            walk("sync", {**base, "sync": sync})

        @block.gpsimd
        def _(gpsimd):
            walk("gpsimd", {**base, "gpsimd": gpsimd})

    return nc


def kernel(L0_attn_x, L0_ffn_w_gate, L0_ffn_w_up, L0_ffn_w_down):
    x = np.ascontiguousarray(np.asarray(L0_attn_x, dtype=np.float32))
    wgT = np.ascontiguousarray(np.asarray(L0_ffn_w_gate, dtype=np.int32).T)
    wuT = np.ascontiguousarray(np.asarray(L0_ffn_w_up, dtype=np.int32).T)
    wdT = np.ascontiguousarray(np.asarray(L0_ffn_w_down, dtype=np.int32).T)
    ident = np.eye(128, dtype=ml_dtypes.bfloat16)

    nc = build_program()
    in_maps = []
    for core in range(8):
        in_maps.append({
            "x": x[core * NB:(core + 1) * NB],
            "wgT": wgT, "wuT": wuT, "wdT": wdT,
            "ident": ident,
        })
    res = run_bass_kernel_spmd(nc, in_maps, core_ids=list(range(8)))
    out = np.concatenate([res.results[i]["out"] for i in range(8)], axis=0)
    if DEBUG_OUTPUTS:
        kernel.debug = [{k: np.asarray(res.results[i]["dbg_" + k])
                         for k in DEBUG_OUTPUTS} for i in range(8)]
    return out


# version 40
# speedup vs baseline: 1.1843x; 1.0017x over previous
import contextlib

import numpy as np
import ml_dtypes
import concourse.bass as bass
import concourse.mybir as mybir
from concourse.bass_utils import run_bass_kernel_spmd

F32 = mybir.dt.float32
BF16 = mybir.dt.bfloat16
I32 = mybir.dt.int32
AF = mybir.ActivationFunctionType
ALU = mybir.AluOpType
AX = mybir.AxisListType

NB = 2
T = 2048
D = 1024
H = 3072
NTB = T // 128
NDT = D // 128
NHT = H // 128
TSB = 512
NTSB = T // TSB
EPS = 1e-5
INV127 = float(np.float32(1.0) / np.float32(127.0))
SCALE_G = INV127
SCALE_Y = INV127 * INV127

DEBUG_OUTPUTS = []


class Ctx:

    def __init__(self, live, proxies, sems, stash):
        self.live = live
        self.prox = proxies
        self.sems = sems
        self.cnt = {k: 0 for k in sems}
        self.stash = stash

    def wait(self, eng, sem, val):
        if val is None or val <= 0:
            return
        if self.live == eng:
            self.prox[eng].wait_ge(self.sems[sem], val)

    def do(self, eng, method, *a, inc=None, incval=1, **kw):
        if self.live == eng:
            ins = getattr(self.prox[eng], method)(*a, **kw)
            if inc is not None:
                ins.then_inc(self.sems[inc], incval)
        if inc is not None:
            self.cnt[inc] += incval

    def dma(self, eng, out, in_, sem, **kw):
        self.do(eng, "dma_start", inc=sem, incval=16, out=out, in_=in_, **kw)

    def put(self, key):
        if key not in self.stash:
            self.stash[key] = dict(self.cnt)

    def at(self, key, sem):
        return self.stash[key][sem]


class DStream:

    def __init__(self, c: Ctx, name, n):
        self.c, self.name, self.n = c, name, n
        self.guard = [None] * n
        self.items = {}
        self.i = 0

    @staticmethod
    def sem_names(name, n):
        return [f"{name}{i}" for i in range(n)]

    def load(self, eng, key, out_of_slot, in_ap, **kw):
        s = self.i % self.n
        self.i += 1
        g = self.guard[s]
        if g:
            self.c.wait(eng, g[0], g[1])
        sem = f"{self.name}{s}"
        self.c.dma(eng, out_of_slot(s), in_ap, sem, **kw)
        self.items[key] = (sem, self.c.cnt[sem])
        return s

    def consumed(self, slot, sem, val):
        self.guard[slot] = (sem, val)

    def wait_item(self, eng, key):
        sem, v = self.items[key]
        self.c.wait(eng, sem, v)

    def totals(self):
        out = {}
        for s in range(self.n):
            out[f"{self.name}{s}"] = self.c.cnt[f"{self.name}{s}"]
        return out


def build_program():
    nc = bass.Bass("TRN2", target_bir_lowering=False)

    x_hbm = nc.dram_tensor("x", [NB, T, D], F32, kind="ExternalInput").ap()
    wgT_hbm = nc.dram_tensor("wgT", [D, H], I32, kind="ExternalInput").ap()
    wuT_hbm = nc.dram_tensor("wuT", [D, H], I32, kind="ExternalInput").ap()
    wdT_hbm = nc.dram_tensor("wdT", [H, D], I32, kind="ExternalInput").ap()
    ident_hbm = nc.dram_tensor("ident", [128, 128], BF16, kind="ExternalInput").ap()
    out_hbm = nc.dram_tensor("out", [NB, T, D], F32, kind="ExternalOutput").ap()

    wg_scr = nc.dram_tensor("wg_scr", [D, H], BF16).ap()
    wu_scr = nc.dram_tensor("wu_scr", [D, H], BF16).ap()
    s_scr = nc.dram_tensor("s_scr", [NB, T, D], BF16).ap()

    dbg = {}
    for name in DEBUG_OUTPUTS:
        shp = {"xn": [NB, T, D], "xnT": [NB, D, T], "s": [NB, T, D]}[name]
        dt = {"xn": F32, "xnT": F32, "s": BF16}[name]
        dbg[name] = nc.dram_tensor("dbg_" + name, shp, dt, kind="ExternalOutput").ap()

    sem_names = (
        ["idld", "cvt", "cvtw", "bn", "rstd", "xnw",
         "Ttr", "Tev", "S", "mx", "E", "rec", "O", "sevc",
         "G", "U", "sG", "hact", "Y", "gev", "sgm", "oadd", "dbgs"]
        + DStream.sem_names("xld", 3)
        + DStream.sem_names("cvc", 2)
        + DStream.sem_names("cwc", 2)
        + DStream.sem_names("cvo", 2)
        + DStream.sem_names("spl", 2)
        + DStream.sem_names("srl", 2)
        + DStream.sem_names("xfl", 2)
        + DStream.sem_names("wgl", 2)
        + DStream.sem_names("wul", 2)
        + DStream.sem_names("ost", 2)
    )

    shared_stash = {}

    with contextlib.ExitStack() as stack:
        ent = stack.enter_context
        wdT_sb = ent(nc.sbuf_tensor("wdT_sb", [128, NHT * 1024], BF16))
        xn_sb = ent(nc.sbuf_tensor("xn_sb", [128, NTB * 1024], BF16))
        xnT_sb = ent(nc.sbuf_tensor("xnT_sb", [128, NDT * 2048], BF16))
        hact = ent(nc.sbuf_tensor("hact", [128, NHT * 512], BF16))
        Ebuf = ent(nc.sbuf_tensor("Ebuf", [128, 2 * 2048], BF16))
        aT = ent(nc.sbuf_tensor("aT", [128, 2048], BF16))
        wch = ent(nc.sbuf_tensor("wch", [128, 4 * 2048], BF16))
        xbuf = ent(nc.sbuf_tensor("xbuf", [128, 3 * 1024], F32))
        xfbuf = ent(nc.sbuf_tensor("xfbuf", [128, 2 * 1024], F32))
        outbuf = ent(nc.sbuf_tensor("outbuf", [128, 2 * 512], F32))
        sqb = ent(nc.sbuf_tensor("sqb", [128, 2 * 1024], BF16))
        srlb = ent(nc.sbuf_tensor("srlb", [128, 2 * 1024], BF16))
        gbuf = ent(nc.sbuf_tensor("gbuf", [128, 4 * 1024], BF16))
        sgbuf = ent(nc.sbuf_tensor("sgbuf", [128, 1024], BF16))
        sGb = ent(nc.sbuf_tensor("sGb", [128, 512], BF16))
        identt = ent(nc.sbuf_tensor("identt", [128, 128], BF16))
        mvAll = ent(nc.sbuf_tensor("mvAll", [128, 32], F32))
        bnst = ent(nc.sbuf_tensor("bnst", [128, 12], F32))
        veps = ent(nc.sbuf_tensor("veps", [128, 16], F32))
        lnt = ent(nc.sbuf_tensor("lnt", [128, 16], F32))
        rstd16 = ent(nc.sbuf_tensor("rstd16", [128, 16], F32))
        nmr16 = ent(nc.sbuf_tensor("nmr16", [128, 16], F32))
        nmx = ent(nc.sbuf_tensor("nmx", [128, 2], F32))
        sume = ent(nc.sbuf_tensor("sume", [128, 2], F32))
        recq = ent(nc.sbuf_tensor("recq", [128, 2], F32))
        psT = ent(nc.psum_tensor("psT", [128, 2048], BF16))
        psF = ent(nc.psum_tensor("psF", [128, 3072], F32))
        sems = {n: ent(nc.semaphore(n)) for n in sem_names}
        block = ent(nc.Block())

        def walk(live, proxies):
            c = Ctx(live, proxies, sems, shared_stash)
            mv2 = mvAll[:].rearrange("p (t two) -> p t two", two=2)
            mv_mean = mv2[:, :, 0]
            mv_var = mv2[:, :, 1]

            psS = psF[:, 0:2048]
            psG = psF[:, 0:1024]
            psU = psF[:, 1024:2048]
            psO = psF[:, 2048:3072]
            psY = psO

            xs = DStream(c, "xld", 3)
            spl = DStream(c, "spl", 2)
            srs = DStream(c, "srl", 2)
            xfs = DStream(c, "xfl", 2)
            wgs = DStream(c, "wgl", 2)
            wus = DStream(c, "wul", 2)
            obs = DStream(c, "ost", 2)
            Tstate = {"g": 0}
            Ehist = []

            c.dma("sync", identt[:], ident_hbm, "idld")

            stg_i32 = [hact[:, 0:6144].bitcast(I32), hact[:, 6144:12288].bitcast(I32)]
            cw_bf = [wch[:, 0:3072], wch[:, 3072:6144]]
            N_CONV = 16
            conv_state = {"loads": [], "done": 0, "out": 0}
            wd_state = {"done": 0, "loaded": 0, "lsem": []}

            def conversions_in():
                for k in range(16):
                    par = k % 2
                    if k >= 2:
                        c.wait("gpsimd", "cvt", k - 1)
                    src = (wgT_hbm, wuT_hbm)[k // 8]
                    ch = k % 8
                    c.dma("gpsimd", stg_i32[par],
                          src[ch * 128:(ch + 1) * 128, :], f"cvc{par}")
                    conv_state["loads"].append((f"cvc{par}", c.cnt[f"cvc{par}"]))

            def conv_cvt_step(n):
                for _ in range(n):
                    k = conv_state["done"]
                    if k >= N_CONV:
                        return
                    if 2 <= k < 16 and conv_state["out"] < k - 1:
                        return
                    conv_state["done"] += 1
                    par = k % 2
                    sem, v = conv_state["loads"][k]
                    c.wait("vector", sem, v)
                    if k >= 2:
                        c.wait("vector", f"cvo{par}", 16 * ((k - par) // 2))
                    c.do("vector", "tensor_copy", cw_bf[par], stg_i32[par],
                         inc="cvt")
                    if conv_state["done"] == N_CONV:
                        c.put("conv_all")

            stg_wd = [srlb[:, 0:2048].bitcast(I32), gbuf[:, 0:2048].bitcast(I32)]

            def conv_wd_step(n):
                for _ in range(n):
                    ht = wd_state["done"]
                    if ht >= NHT:
                        return
                    wd_state["done"] += 1
                    while wd_state["loaded"] <= min(ht + 1, NHT - 1):
                        lt = wd_state["loaded"]
                        wd_state["loaded"] += 1
                        sl = lt % 2
                        if lt >= 2:
                            c.wait("gpsimd", "cvtw", lt - 1)
                        c.dma("gpsimd", stg_wd[sl],
                              wdT_hbm[lt * 128:(lt + 1) * 128, :], f"cwc{sl}")
                        wd_state["lsem"].append((f"cwc{sl}", c.cnt[f"cwc{sl}"]))
                    sem, v = wd_state["lsem"][ht]
                    c.wait("gpsimd", sem, v)
                    c.do("gpsimd", "tensor_copy",
                         wdT_sb[:, ht * 1024:(ht + 1) * 1024],
                         stg_wd[ht % 2], inc="cvtw")

            def conv_out_step():
                while conv_state["out"] < min(conv_state["done"], 16):
                    k = conv_state["out"]
                    conv_state["out"] += 1
                    par = k % 2
                    c.wait("scalar", "cvt", k + 1)
                    dst = (wg_scr, wu_scr)[k // 8]
                    ch = k % 8
                    c.dma("scalar", dst[ch * 128:(ch + 1) * 128, :], cw_bf[par],
                          f"cvo{par}")
                    if conv_state["out"] == 16:
                        c.put("conv_done")

            def conv_wait(eng):
                c.wait(eng, "cvo0", c.at("conv_done", "cvo0"))
                c.wait(eng, "cvo1", c.at("conv_done", "cvo1"))

            def ln_phase(b, as_steps=False):
                steps = []

                def step(fn):
                    if as_steps:
                        steps.append(fn)
                    else:
                        fn()

                def x_eng(tb):
                    return "scalar" if (b == 0 and tb % 2 == 1) else "sync"

                def p1_tile(tb):
                    par = xs.load(x_eng(tb), ("p1", b, tb),
                                  lambda s: xbuf[:, s * 1024:(s + 1) * 1024],
                                  x_hbm[b, tb * 128:(tb + 1) * 128, :])
                    xs.wait_item("vector", ("p1", b, tb))
                    c.wait("vector", "bn", c.cnt["bn"])
                    c.do("vector", "bn_stats", bnst[:, 0:6],
                         xbuf[:, par * 1024: par * 1024 + 512], inc="bn")
                    c.wait("vector", "bn", c.cnt["bn"])
                    c.do("vector", "bn_stats", bnst[:, 6:12],
                         xbuf[:, par * 1024 + 512: par * 1024 + 1024], inc="bn")
                    c.wait("vector", "bn", c.cnt["bn"])
                    c.do("vector", "bn_aggr", mvAll[:, 2 * tb: 2 * tb + 2],
                         bnst[:, 0:12], inc="bn")
                    xs.consumed(par, "bn", c.cnt["bn"])

                nmr_holder = [None]

                def rstd_block():
                    c.wait("vector", "bn", c.cnt["bn"])
                    c.do("vector", "tensor_scalar_add", veps[:, 0:16], mv_var,
                         EPS, inc="bn")
                    c.wait("scalar", "bn", c.cnt["bn"])
                    c.do("scalar", "activation", lnt[:, 0:16], veps[:, 0:16],
                         AF.Ln, inc="rstd")
                    c.wait("scalar", "rstd", c.cnt["rstd"])
                    c.do("scalar", "activation", rstd16[:, 0:16], lnt[:, 0:16],
                         AF.Exp, scale=-0.5, inc="rstd")
                    c.wait("vector", "rstd", c.cnt["rstd"])
                    c.do("vector", "scalar_tensor_tensor", nmr16[:, 0:16],
                         mv_mean, -1.0, rstd16[:, 0:16], ALU.mult, ALU.mult,
                         inc="bn")
                    nmr_holder[0] = c.cnt["bn"]

                def p2_tile(tb):
                    eng = "gpsimd"
                    par = xs.load(x_eng(tb), ("p2", b, tb),
                                  lambda s: xbuf[:, s * 1024:(s + 1) * 1024],
                                  x_hbm[b, tb * 128:(tb + 1) * 128, :])
                    xs.wait_item(eng, ("p2", b, tb))
                    c.wait(eng, "bn", nmr_holder[0])
                    if b >= 1 and tb == 0:
                        c.wait(eng, "O", c.at(("attn", b - 1), "O"))
                    c.do(eng, "tensor_scalar",
                         xn_sb[:, tb * 1024:(tb + 1) * 1024],
                         xbuf[:, par * 1024:(par + 1) * 1024],
                         rstd16[:, tb:tb + 1], nmr16[:, tb:tb + 1],
                         ALU.mult, ALU.add, inc="xnw")
                    xs.consumed(par, "xnw", c.cnt["xnw"])
                    if b == 0:
                        xnT_tile(b, tb)
                        if tb % 4 == 3:
                            S0_chunk(b, tb // 4)
                    if tb == NTB - 1:
                        c.put(("xnw", b))

                def rstd_half(lo, hi):
                    c.wait("vector", "bn", c.cnt["bn"])
                    c.do("vector", "tensor_scalar_add", veps[:, lo:hi],
                         mv_var[:, lo:hi], EPS, inc="bn")
                    c.wait("scalar", "bn", c.cnt["bn"])
                    c.do("scalar", "activation", lnt[:, lo:hi], veps[:, lo:hi],
                         AF.Ln, inc="rstd")
                    c.wait("scalar", "rstd", c.cnt["rstd"])
                    c.do("scalar", "activation", rstd16[:, lo:hi], lnt[:, lo:hi],
                         AF.Exp, scale=-0.5, inc="rstd")
                    c.wait("vector", "rstd", c.cnt["rstd"])
                    c.do("vector", "scalar_tensor_tensor", nmr16[:, lo:hi],
                         mv_mean[:, lo:hi], -1.0, rstd16[:, lo:hi],
                         ALU.mult, ALU.mult, inc="bn")
                    nmr_holder[0] = c.cnt["bn"]

                if b == 0 and not as_steps:
                    for tb in range(2):
                        p1_tile(tb)
                    rstd_half(0, 2)
                    for pr in range(1, 8):
                        for tb in range(2):
                            p1_tile(2 * pr + tb)
                            p2_tile(2 * (pr - 1) + tb)
                        rstd_half(2 * pr, 2 * pr + 2)
                    for tb in range(14, NTB):
                        p2_tile(tb)
                    softmax_q(0, 0, embed_tr=False)
                    return None
                for tb in range(NTB):
                    step(lambda tb=tb: p1_tile(tb))
                step(rstd_block)
                for tb in range(NTB):
                    step(lambda tb=tb: p2_tile(tb))
                if as_steps:
                    return steps

            def xnT_tile(b, tb):
                if tb == 0:
                    c.put(("xnTbase", b))
                grp = Tstate["g"]
                bank = (grp % 2) * 1024
                c.wait("tensor", "Tev", grp - 1)
                c.wait("tensor", "xnw", b * NTB + tb + 1)
                if grp == 0:
                    c.wait("tensor", "idld", 16)
                for dt in range(NDT):
                    c.do("tensor", "transpose",
                         psT[:, bank + dt * 128: bank + (dt + 1) * 128],
                         xn_sb[:, tb * 1024 + dt * 128: tb * 1024 + (dt + 1) * 128],
                         identt[:], inc="Ttr" if dt == NDT - 1 else None)
                c.wait("vector", "Ttr", c.cnt["Ttr"])
                if b >= 1 and tb == 0:
                    c.wait("vector", "G", c.cnt["G"])
                    c.wait("vector", "U", c.cnt["U"])
                o = xnT_sb[:].bitcast(I32).rearrange(
                    "p (dt t) -> p dt t", dt=NDT)[:, :, tb * 64:(tb + 1) * 64]
                i_ = psT[:, bank: bank + NDT * 128].bitcast(I32).rearrange(
                    "p (dt t) -> p dt t", dt=NDT)
                c.do("vector", "tensor_copy", o, i_, inc="Tev")
                Tstate["g"] += 1
                if tb == NTB - 1:
                    c.put(("xnT", b))

            def S0_chunk(b, kc):
                if kc == 0:
                    c.wait("tensor", "E", c.cnt["E"])
                    if b > 0:
                        c.wait("tensor", "sG", c.cnt["sG"])
                        c.wait("tensor", "hact", c.cnt["hact"])
                base = c.at(("xnTbase", b), "Tev")
                c.wait("tensor", "Tev", base + 4 * (kc + 1))
                for dt in range(NDT):
                    c.do("tensor", "matmul",
                         psS[:, kc * 512:(kc + 1) * 512],
                         xnT_sb[:, dt * 2048: dt * 2048 + 128],
                         xnT_sb[:, dt * 2048 + kc * 512: dt * 2048 + (kc + 1) * 512],
                         start=(dt == 0), stop=(dt == NDT - 1),
                         inc="S" if (dt == NDT - 1 and kc == 3) else None)

            def tr_group(b, qq, g2):
                grp = Tstate["g"]
                bank = (grp % 2) * 1024
                e0 = (qq % 2) * 2048
                c.wait("tensor", "Tev", grp - 1)
                if g2 == 0:
                    c.wait("tensor", "E", c.at(("E", b, qq), "E"))
                for i in range(8):
                    kt = g2 * 8 + i
                    c.do("tensor", "transpose",
                         psT[:, bank + i * 128: bank + (i + 1) * 128],
                         Ebuf[:, e0 + kt * 128: e0 + (kt + 1) * 128],
                         identt[:], inc="Ttr" if i == 7 else None)
                if g2 == 1:
                    c.put(("Etr", b, qq))
                    Ehist.append(("Etr", b, qq))
                return bank

            def tr_evac(b, qq, g2, bank):
                c.wait("vector", "Ttr", c.cnt["Ttr"])
                c.do("vector", "tensor_copy",
                     aT[:, g2 * 1024:(g2 + 1) * 1024].bitcast(I32),
                     psT[:, bank: bank + 1024].bitcast(I32), inc="Tev")
                c.put(("aTev", b, qq, g2))
                Tstate["g"] += 1

            def softmax_q(b, q, embed_tr):
                if q == 0 and b > 0:
                    c.do("scalar", "activation", lnt[:, 0:1], identt[:, 0:1],
                         AF.Exp)
                c.wait("vector", "S", c.cnt["S"])
                if q >= 2:
                    c.wait("vector", "sevc", c.cnt["sevc"])
                c.do("vector", "tensor_reduce", nmx[:, q % 2: q % 2 + 1],
                     psS[:, 0:2048], AX.X, ALU.max, negate=True, inc="mx")
                if embed_tr:
                    bankB = tr_group(b, q - 1, 1)
                    tr_evac(b, q - 1, 1, bankB)
                c.wait("scalar", "mx", c.cnt["mx"])
                prev_same = [k for k in Ehist if k[2] % 2 == q % 2]
                if prev_same:
                    c.wait("scalar", "Ttr", c.at(prev_same[-1], "Ttr"))
                if q >= 2:
                    c.wait("scalar", "rec", c.cnt["rec"] - 1)
                c.do("scalar", "activation",
                     Ebuf[:, (q % 2) * 2048:(q % 2 + 1) * 2048],
                     psS[:, 0:2048], AF.Exp,
                     bias=nmx[:, q % 2: q % 2 + 1],
                     accum_out=sume[:, q % 2: q % 2 + 1], inc="E")
                c.wait("scalar", "E", c.cnt["E"])
                c.do("scalar", "copy", bnst[:, q % 2: q % 2 + 1],
                     sume[:, q % 2: q % 2 + 1], inc="E")
                c.wait("vector", "E", c.cnt["E"])
                c.do("vector", "reciprocal", recq[:, q % 2: q % 2 + 1],
                     bnst[:, q % 2: q % 2 + 1], inc="rec")
                c.put(("E", b, q))
                c.put(("rec", b, q))

            def attn_phase(b, leftover=None):
                lo = leftover if leftover is not None else []
                assert ("xnT", b) in c.stash

                for q in range(NTB + 1):
                    qq = q - 1
                    conv_out_step()
                    conv_wd_step(2)
                    if q == 0 or q == NTB:
                        conv_cvt_step(3)
                    if q < NTB:
                        if q >= 1:
                            c.wait("tensor", "E", c.cnt["E"])
                            for dt in range(NDT):
                                for kc in range(4):
                                    c.do("tensor", "matmul",
                                         psS[:, kc * 512:(kc + 1) * 512],
                                         xnT_sb[:, dt * 2048 + q * 128: dt * 2048 + (q + 1) * 128],
                                         xnT_sb[:, dt * 2048 + kc * 512: dt * 2048 + (kc + 1) * 512],
                                         start=(dt == 0), stop=(dt == NDT - 1),
                                         inc="S" if (dt == NDT - 1 and kc == 3) else None)
                    if q >= 1:
                        bankA = tr_group(b, qq, 0)
                        tr_evac(b, qq, 0, bankA)
                    if q < NTB and q >= 1:
                        softmax_q(b, q, embed_tr=True)
                        conv_cvt_step(3)
                        for _ in range(2):
                            if lo:
                                lo.pop(0)()
                    if q >= 1:
                        if q == NTB:
                            bankB = tr_group(b, qq, 1)
                            tr_evac(b, qq, 1, bankB)
                        c.wait("tensor", "sevc", c.cnt["sevc"])
                        if b > 0 and qq == 0:
                            c.wait("tensor", "gev", c.at(("ffn", b - 1), "gev"))
                        for g2 in range(2):
                            c.wait("tensor", "Tev", c.at(("aTev", b, qq, g2), "Tev"))
                            for i in range(8):
                                kt = g2 * 8 + i
                                for dh in range(2):
                                    c.do("tensor", "matmul",
                                         psO[:, dh * 512:(dh + 1) * 512],
                                         aT[:, kt * 128:(kt + 1) * 128],
                                         xn_sb[:, kt * 1024 + dh * 512:
                                               kt * 1024 + (dh + 1) * 512],
                                         start=(kt == 0), stop=(kt == 15),
                                         inc="O" if (kt == 15 and dh == 1) else None)
                        c.wait("scalar", "O", c.cnt["O"])
                        c.wait("scalar", "rec", c.at(("rec", b, qq), "rec"))
                        spq = ("spl", b, qq - 2)
                        if spq in spl.items:
                            sem, v = spl.items[spq]
                            c.wait("scalar", sem, v)
                        elif qq < 2 and b > 0:
                            sem, v = spl.items[("spl", b - 1, qq - 2 + NTB)]
                            c.wait("scalar", sem, v)
                        c.do("scalar", "activation",
                             sqb[:, (qq % 2) * 1024:(qq % 2 + 1) * 1024],
                             psO[:, 0:1024], AF.Copy,
                             scale=recq[:, qq % 2: qq % 2 + 1], inc="sevc")
                        c.wait("scalar", "sevc", c.cnt["sevc"])
                        spl.load("scalar", ("spl", b, qq),
                                 lambda s: s_scr[b, qq * 128:(qq + 1) * 128, :],
                                 sqb[:, (qq % 2) * 1024:(qq % 2 + 1) * 1024])
                        if qq == 12:
                            load_chunk(b, 0)
                            load_chunk(b, 1)
                while conv_state["done"] < N_CONV or conv_state["out"] < 16:
                    conv_cvt_step(1)
                    conv_out_step()
                conv_wd_step(NHT)
                c.put(("attn", b))

            wg3 = wg_scr.rearrange("(dt p) hh -> p dt hh", p=128)
            wu3 = wu_scr.rearrange("(dt p) hh -> p dt hh", p=128)

            def load_chunk(b, lidx):
                if lidx >= NTSB * 12:
                    return
                hp = lidx % 12
                if b == 0 and lidx == 0:
                    conv_wait("scalar")
                    conv_wait("sync")
                wgs.load("scalar", ("wg", b, lidx),
                         lambda s: wch[:, s * 2048:(s + 1) * 2048].rearrange(
                             "p (dt h) -> p dt h", h=256),
                         wg3[:, :, hp * 256:(hp + 1) * 256])
                wus.load("sync", ("wu", b, lidx),
                         lambda s: wch[:, (2 + s) * 2048:(3 + s) * 2048].rearrange(
                             "p (dt h) -> p dt h", h=256),
                         wu3[:, :, hp * 256:(hp + 1) * 256])

            def ffn_phase(b, bg_steps=None, epi_tiles=None):
                bg = list(bg_steps) if bg_steps else []
                epi = list(epi_tiles) if epi_tiles else []
                ybank_guard = {"psO": 0, "psU": 0}
                finals = []

                def queue_final(j, tbp, tb):
                    tbg = j * 4 + tbp * 2 + tb
                    gi = b * NTB + tbg
                    par = gi % 2
                    gev_need = c.at(("gevp", b, j, tbp), "gev")

                    def do_loads():
                        spl.wait_item("sync", ("spl", b, tbg))
                        sslot = srs.load("sync", ("srl", gi),
                                         lambda s: srlb[:, s * 1024:(s + 1) * 1024],
                                         s_scr[b, tbg * 128:(tbg + 1) * 128, :])
                        srs.consumed(sslot, "sgm", gi + 1)
                        xslot = xfs.load("sync", ("xfl", gi),
                                         lambda s: xfbuf[:, s * 1024:(s + 1) * 1024],
                                         x_hbm[b, tbg * 128:(tbg + 1) * 128, :])
                        xfs.consumed(xslot, "oadd", 2 * (gi + 1))
                        finals.insert(0, do_mul)

                    def do_mul():
                        meng = "vector" if gi >= 2 * NTB - 4 else "gpsimd"
                        c.wait(meng, "gev", gev_need)
                        srs.wait_item(meng, ("srl", gi))
                        if gi >= 1:
                            c.wait(meng, "oadd", 2 * gi)
                        c.do(meng, "tensor_tensor",
                             sgbuf[:, 0:1024],
                             srlb[:, par * 1024:(par + 1) * 1024],
                             gbuf[:, (gi % 4) * 1024:(gi % 4 + 1) * 1024],
                             ALU.mult, inc="sgm")
                        sgm_need = c.cnt["sgm"]

                        def do_adds():
                            oadds = []
                            tail = gi >= 2 * NTB - 4
                            obuf = (xnT_sb[:].bitcast(F32)
                                    [:, (gi % 4) * 1024:(gi % 4 + 1) * 1024]
                                    if tail else outbuf[:, 0:1024])
                            for half in range(2):
                                c.wait("vector", "sgm", sgm_need)
                                xfs.wait_item("vector", ("xfl", gi))
                                osem = f"ost{half}"
                                if not tail:
                                    c.wait("vector", osem, c.cnt[osem])
                                c.do("vector", "tensor_tensor",
                                     obuf[:, half * 512:(half + 1) * 512],
                                     sgbuf[:, half * 512:(half + 1) * 512],
                                     xfbuf[:, par * 1024 + half * 512:
                                           par * 1024 + (half + 1) * 512],
                                     ALU.add, inc="oadd")
                                oadds.append(c.cnt["oadd"])

                            def do_stores():
                                for half in range(2):
                                    osem = f"ost{half}"
                                    c.wait("scalar", "oadd", oadds[half])
                                    c.dma("scalar",
                                          out_hbm[b, tbg * 128:(tbg + 1) * 128,
                                                  half * 512:(half + 1) * 512],
                                          obuf[:, half * 512:(half + 1) * 512],
                                          osem)
                            finals.insert(0, do_stores)
                        finals.insert(0, do_adds)
                    finals.append(do_loads)

                def drain_final(n=1):
                    for _ in range(n):
                        if finals:
                            finals.pop(0)()

                for j in range(NTSB):
                    for hp in range(12):
                        gidx = j * 12 + hp
                        buf = gidx % 2
                        for hh in range(2):
                            ht = hp * 2 + hh
                            c.wait("tensor", "sG", c.cnt["sG"] - 1)
                            c.wait("tensor", "hact", c.cnt["hact"] - 1)
                            if j == 0 and ht == 0:
                                c.wait("tensor", "E", c.at(("attn", b), "E"))
                            wgs.wait_item("tensor", ("wg", b, gidx))
                            wus.wait_item("tensor", ("wu", b, gidx))
                            hb = (ht % 2) * 512
                            for dt in range(NDT):
                                c.do("tensor", "matmul", psG[:, hb: hb + 512],
                                     wch[:, buf * 2048 + dt * 256 + hh * 128:
                                         buf * 2048 + dt * 256 + (hh + 1) * 128],
                                     xnT_sb[:, dt * 2048 + j * 512: dt * 2048 + (j + 1) * 512],
                                     start=(dt == 0), stop=(dt == NDT - 1),
                                     inc="G" if dt == NDT - 1 else None)
                            if ht == 0:
                                c.wait("tensor", "gev", ybank_guard["psU"])
                            for dt in range(NDT):
                                c.do("tensor", "matmul", psU[:, hb: hb + 512],
                                     wch[:, (2 + buf) * 2048 + dt * 256 + hh * 128:
                                         (2 + buf) * 2048 + dt * 256 + (hh + 1) * 128],
                                     xnT_sb[:, dt * 2048 + j * 512: dt * 2048 + (j + 1) * 512],
                                     start=(dt == 0), stop=(dt == NDT - 1),
                                     inc="U" if dt == NDT - 1 else None)
                            if j == 0 and hp == 0 and hh == 0:
                                c.do("scalar", "activation", lnt[:, 0:1],
                                     identt[:, 0:1], AF.Silu)
                            c.wait("scalar", "G", c.cnt["G"])
                            c.wait("scalar", "hact", c.cnt["hact"])
                            c.do("scalar", "activation",
                                 sGb[:, 0:512],
                                 psG[:, hb: hb + 512], AF.Silu, scale=SCALE_G, inc="sG")
                            c.wait("vector", "U", c.cnt["U"])
                            c.wait("vector", "sG", c.cnt["sG"])
                            if b > 0 or j > 0:
                                key = ("Ytot", b, j - 1) if j > 0 else ("Ytot", b - 1, NTSB - 1)
                                c.wait("vector", "Y", c.at(key, "Y"))
                            if b == 0 and j == 0 and ht == 0:
                                c.wait("vector", "cvt", c.at("conv_all", "cvt"))
                                c.wait("vector", "cvtw", NHT)
                            c.do("vector", "scalar_tensor_tensor",
                                 hact[:, ht * 512:(ht + 1) * 512],
                                 psU[:, hb: hb + 512], 1.0,
                                 sGb[:, 0:512],
                                 ALU.mult, ALU.mult, inc="hact")
                            c.put(("hactd", b, j, ht))
                        c.put(("wchG", b, gidx))
                        wgs.consumed(buf, "G", c.at(("wchG", b, gidx), "G"))
                        wus.consumed(buf, "U", c.at(("wchG", b, gidx), "U"))
                        load_chunk(b, gidx + 2)
                        drain_final(2)
                    for tbp in range(2):
                        for dh in range(2):
                            borrow = (dh == 1 and (j != NTSB - 1 or not epi))
                            ybank = psU if borrow else psY
                            guard_key = "psU" if borrow else "psO"
                            c.wait("tensor", "gev", ybank_guard[guard_key])
                            if j == 0 and tbp == 0 and dh == 0:
                                c.wait("tensor", "sevc", c.at(("attn", b), "sevc"))
                                if b == 0:
                                    c.wait("tensor", "cvtw", NHT)
                            if borrow and tbp == 0:
                                c.wait("tensor", "hact",
                                       c.at(("hactd", b, j, NHT - 1), "hact"))
                            for ht in range(NHT):
                                if tbp == 0 and dh == 0:
                                    c.wait("tensor", "hact",
                                           c.at(("hactd", b, j, ht), "hact"))
                                for tb in range(2):
                                    c.do("tensor", "matmul",
                                         ybank[:, tb * 512:(tb + 1) * 512],
                                         hact[:, ht * 512 + (tbp * 2 + tb) * 128:
                                              ht * 512 + (tbp * 2 + tb + 1) * 128],
                                         wdT_sb[:, ht * 1024 + dh * 512:
                                                ht * 1024 + (dh + 1) * 512],
                                         start=(ht == 0), stop=(ht == NHT - 1),
                                         inc="Y" if (ht == NHT - 1 and tb == 1) else None)
                            if j == NTSB - 1 and epi:
                                for _ in range(4):
                                    if epi:
                                        epi.pop(0)()
                            c.wait("scalar", "Y", c.cnt["Y"])
                            for tb in range(2):
                                tbg = j * 4 + tbp * 2 + tb
                                gi = b * NTB + tbg
                                if dh == 0 and gi >= 4:
                                    c.wait("scalar", "sgm", gi - 3)
                                c.do("scalar", "activation",
                                     gbuf[:, (gi % 4) * 1024 + dh * 512:
                                          (gi % 4) * 1024 + (dh + 1) * 512],
                                     ybank[:, tb * 512:(tb + 1) * 512], AF.Silu,
                                     scale=SCALE_Y, inc="gev")
                            ybank_guard[guard_key] = c.cnt["gev"]
                            if j == NTSB - 1 and tbp == 1 and dh == 1:
                                while epi:
                                    epi.pop(0)()
                            if b == NB - 1 and j == NTSB - 1:
                                drain_final(3)
                            for _ in range(3):
                                if bg:
                                    bg.pop(0)()
                        c.put(("gevp", b, j, tbp))
                        for tb in range(2):
                            queue_final(j, tbp, tb)
                    c.put(("Ytot", b, j))
                c.put(("ffn", b))
                return finals

            ln_phase(0)
            conversions_in()
            attn_phase(0)
            lo1 = ffn_phase(0, bg_steps=ln_phase(1, as_steps=True),
                            epi_tiles=[
                                (lambda tb=tb: (
                                    xnT_tile(1, tb),
                                    S0_chunk(1, tb // 4) if tb % 4 == 3 else None))
                                for tb in range(NTB)]
                            + [lambda: softmax_q(1, 0, embed_tr=False)])
            attn_phase(1, leftover=lo1)
            rest = ffn_phase(1)
            while rest:
                rest.pop(0)()

            if "s" in dbg:
                for b in range(NB):
                    for qq in range(NTB):
                        spl.wait_item("gpsimd", ("spl", b, qq))
                c.dma("gpsimd", dbg["s"][:, :, :], s_scr[:, :, :], "dbgs")
            if "xn" in dbg:
                for tb in range(NTB):
                    c.dma("gpsimd", dbg["xn"][NB - 1, tb * 128:(tb + 1) * 128, :],
                          xn_sb[:, tb * 1024:(tb + 1) * 1024], "dbgs")
            if "xnT" in dbg:
                for dt in range(NDT):
                    c.dma("gpsimd", dbg["xnT"][NB - 1, dt * 128:(dt + 1) * 128, :],
                          xnT_sb[:, dt * 2048:(dt + 1) * 2048], "dbgs")

            c.wait("scalar", "ost0", c.cnt["ost0"])
            c.wait("scalar", "ost1", c.cnt["ost1"])
            if dbg:
                c.wait("gpsimd", "dbgs", c.cnt["dbgs"])

        base = {"tensor": None, "vector": None, "scalar": None,
                "sync": None, "gpsimd": None}

        @block.tensor
        def _(tensor):
            walk("tensor", {**base, "tensor": tensor})

        @block.vector
        def _(vector):
            walk("vector", {**base, "vector": vector})

        @block.scalar
        def _(scalar):
            walk("scalar", {**base, "scalar": scalar})

        @block.sync
        def _(sync):
            walk("sync", {**base, "sync": sync})

        @block.gpsimd
        def _(gpsimd):
            walk("gpsimd", {**base, "gpsimd": gpsimd})

    return nc


def kernel(L0_attn_x, L0_ffn_w_gate, L0_ffn_w_up, L0_ffn_w_down):
    x = np.ascontiguousarray(np.asarray(L0_attn_x, dtype=np.float32))
    wgT = np.ascontiguousarray(np.asarray(L0_ffn_w_gate, dtype=np.int32).T)
    wuT = np.ascontiguousarray(np.asarray(L0_ffn_w_up, dtype=np.int32).T)
    wdT = np.ascontiguousarray(np.asarray(L0_ffn_w_down, dtype=np.int32).T)
    ident = np.eye(128, dtype=ml_dtypes.bfloat16)

    nc = build_program()
    in_maps = []
    for core in range(8):
        in_maps.append({
            "x": x[core * NB:(core + 1) * NB],
            "wgT": wgT, "wuT": wuT, "wdT": wdT,
            "ident": ident,
        })
    res = run_bass_kernel_spmd(nc, in_maps, core_ids=list(range(8)))
    out = np.concatenate([res.results[i]["out"] for i in range(8)], axis=0)
    if DEBUG_OUTPUTS:
        kernel.debug = [{k: np.asarray(res.results[i]["dbg_" + k])
                         for k in DEBUG_OUTPUTS} for i in range(8)]
    return out


# version 41 (final)
# speedup vs baseline: 1.1998x; 1.0131x over previous
import contextlib

import numpy as np
import ml_dtypes
import concourse.bass as bass
import concourse.mybir as mybir
from concourse.bass_utils import run_bass_kernel_spmd

F32 = mybir.dt.float32
BF16 = mybir.dt.bfloat16
I32 = mybir.dt.int32
AF = mybir.ActivationFunctionType
ALU = mybir.AluOpType
AX = mybir.AxisListType

NB = 2
T = 2048
D = 1024
H = 3072
NTB = T // 128
NDT = D // 128
NHT = H // 128
TSB = 512
NTSB = T // TSB
EPS = 1e-5
INV127 = float(np.float32(1.0) / np.float32(127.0))
SCALE_G = INV127
SCALE_Y = INV127 * INV127

DEBUG_OUTPUTS = []


class Ctx:

    def __init__(self, live, proxies, sems, stash):
        self.live = live
        self.prox = proxies
        self.sems = sems
        self.cnt = {k: 0 for k in sems}
        self.stash = stash

    def wait(self, eng, sem, val):
        if val is None or val <= 0:
            return
        if self.live == eng:
            self.prox[eng].wait_ge(self.sems[sem], val)

    def do(self, eng, method, *a, inc=None, incval=1, **kw):
        if self.live == eng:
            ins = getattr(self.prox[eng], method)(*a, **kw)
            if inc is not None:
                ins.then_inc(self.sems[inc], incval)
        if inc is not None:
            self.cnt[inc] += incval

    def dma(self, eng, out, in_, sem, **kw):
        self.do(eng, "dma_start", inc=sem, incval=16, out=out, in_=in_, **kw)

    def put(self, key):
        if key not in self.stash:
            self.stash[key] = dict(self.cnt)

    def at(self, key, sem):
        return self.stash[key][sem]


class DStream:

    def __init__(self, c: Ctx, name, n):
        self.c, self.name, self.n = c, name, n
        self.guard = [None] * n
        self.items = {}
        self.i = 0

    @staticmethod
    def sem_names(name, n):
        return [f"{name}{i}" for i in range(n)]

    def load(self, eng, key, out_of_slot, in_ap, **kw):
        s = self.i % self.n
        self.i += 1
        g = self.guard[s]
        if g:
            self.c.wait(eng, g[0], g[1])
        sem = f"{self.name}{s}"
        self.c.dma(eng, out_of_slot(s), in_ap, sem, **kw)
        self.items[key] = (sem, self.c.cnt[sem])
        return s

    def consumed(self, slot, sem, val):
        self.guard[slot] = (sem, val)

    def wait_item(self, eng, key):
        sem, v = self.items[key]
        self.c.wait(eng, sem, v)

    def totals(self):
        out = {}
        for s in range(self.n):
            out[f"{self.name}{s}"] = self.c.cnt[f"{self.name}{s}"]
        return out


def build_program():
    nc = bass.Bass("TRN2", target_bir_lowering=False)

    x_hbm = nc.dram_tensor("x", [NB, T, D], F32, kind="ExternalInput").ap()
    wgT_hbm = nc.dram_tensor("wgT", [D, H], I32, kind="ExternalInput").ap()
    wuT_hbm = nc.dram_tensor("wuT", [D, H], I32, kind="ExternalInput").ap()
    wdT_hbm = nc.dram_tensor("wdT", [H, D], I32, kind="ExternalInput").ap()
    ident_hbm = nc.dram_tensor("ident", [128, 128], BF16, kind="ExternalInput").ap()
    out_hbm = nc.dram_tensor("out", [NB, T, D], F32, kind="ExternalOutput").ap()

    wg_scr = nc.dram_tensor("wg_scr", [D, H], BF16).ap()
    wu_scr = nc.dram_tensor("wu_scr", [D, H], BF16).ap()
    s_scr = nc.dram_tensor("s_scr", [NB, T, D], BF16).ap()

    dbg = {}
    for name in DEBUG_OUTPUTS:
        shp = {"xn": [NB, T, D], "xnT": [NB, D, T], "s": [NB, T, D]}[name]
        dt = {"xn": F32, "xnT": F32, "s": BF16}[name]
        dbg[name] = nc.dram_tensor("dbg_" + name, shp, dt, kind="ExternalOutput").ap()

    sem_names = (
        ["idld", "cvt", "cvtw", "bn", "rstd", "xnw",
         "Ttr", "Tev", "S", "mx", "E", "rec", "O", "sevc",
         "G", "U", "sG", "hact", "Y", "gev", "sgm", "oadd", "dbgs"]
        + DStream.sem_names("xld", 3)
        + DStream.sem_names("xl0", 7)
        + DStream.sem_names("cvc", 2)
        + DStream.sem_names("cwc", 2)
        + DStream.sem_names("cvo", 2)
        + DStream.sem_names("spl", 2)
        + DStream.sem_names("srl", 2)
        + DStream.sem_names("xfl", 2)
        + DStream.sem_names("wgl", 2)
        + DStream.sem_names("wul", 2)
        + DStream.sem_names("ost", 2)
    )

    shared_stash = {}

    with contextlib.ExitStack() as stack:
        ent = stack.enter_context
        wdT_sb = ent(nc.sbuf_tensor("wdT_sb", [128, NHT * 1024], BF16))
        xn_sb = ent(nc.sbuf_tensor("xn_sb", [128, NTB * 1024], BF16))
        xnT_sb = ent(nc.sbuf_tensor("xnT_sb", [128, NDT * 2048], BF16))
        hact = ent(nc.sbuf_tensor("hact", [128, NHT * 512], BF16))
        Ebuf = ent(nc.sbuf_tensor("Ebuf", [128, 2 * 2048], BF16))
        aT = ent(nc.sbuf_tensor("aT", [128, 2048], BF16))
        wch = ent(nc.sbuf_tensor("wch", [128, 4 * 2048], BF16))
        xbuf = ent(nc.sbuf_tensor("xbuf", [128, 3 * 1024], F32))
        xfbuf = ent(nc.sbuf_tensor("xfbuf", [128, 2 * 1024], F32))
        outbuf = ent(nc.sbuf_tensor("outbuf", [128, 2 * 512], F32))
        sqb = ent(nc.sbuf_tensor("sqb", [128, 2 * 1024], BF16))
        srlb = ent(nc.sbuf_tensor("srlb", [128, 2 * 1024], BF16))
        gbuf = ent(nc.sbuf_tensor("gbuf", [128, 4 * 1024], BF16))
        sgbuf = ent(nc.sbuf_tensor("sgbuf", [128, 1024], BF16))
        sGb = ent(nc.sbuf_tensor("sGb", [128, 512], BF16))
        identt = ent(nc.sbuf_tensor("identt", [128, 128], BF16))
        mvAll = ent(nc.sbuf_tensor("mvAll", [128, 32], F32))
        bnst = ent(nc.sbuf_tensor("bnst", [128, 12], F32))
        veps = ent(nc.sbuf_tensor("veps", [128, 16], F32))
        lnt = ent(nc.sbuf_tensor("lnt", [128, 16], F32))
        rstd16 = ent(nc.sbuf_tensor("rstd16", [128, 16], F32))
        nmr16 = ent(nc.sbuf_tensor("nmr16", [128, 16], F32))
        nmx = ent(nc.sbuf_tensor("nmx", [128, 2], F32))
        sume = ent(nc.sbuf_tensor("sume", [128, 2], F32))
        recq = ent(nc.sbuf_tensor("recq", [128, 2], F32))
        psT = ent(nc.psum_tensor("psT", [128, 2048], BF16))
        psF = ent(nc.psum_tensor("psF", [128, 3072], F32))
        sems = {n: ent(nc.semaphore(n)) for n in sem_names}
        block = ent(nc.Block())

        def walk(live, proxies):
            c = Ctx(live, proxies, sems, shared_stash)
            mv2 = mvAll[:].rearrange("p (t two) -> p t two", two=2)
            mv_mean = mv2[:, :, 0]
            mv_var = mv2[:, :, 1]

            psS = psF[:, 0:2048]
            psG = psF[:, 0:1024]
            psU = psF[:, 1024:2048]
            psO = psF[:, 2048:3072]
            psY = psO

            xs = DStream(c, "xld", 3)
            xs0 = DStream(c, "xl0", 7)

            def xslot0(s):
                if s < 3:
                    return xbuf[:, s * 1024:(s + 1) * 1024]
                return wch[:].bitcast(F32)[:, (s - 3) * 1024:(s - 2) * 1024]
            spl = DStream(c, "spl", 2)
            srs = DStream(c, "srl", 2)
            xfs = DStream(c, "xfl", 2)
            wgs = DStream(c, "wgl", 2)
            wus = DStream(c, "wul", 2)
            obs = DStream(c, "ost", 2)
            Tstate = {"g": 0}
            Ehist = []

            c.dma("sync", identt[:], ident_hbm, "idld")

            stg_i32 = [hact[:, 0:6144].bitcast(I32), hact[:, 6144:12288].bitcast(I32)]
            cw_bf = [wch[:, 0:3072], wch[:, 3072:6144]]
            N_CONV = 16
            conv_state = {"loads": [], "done": 0, "out": 0}
            wd_state = {"done": 0, "loaded": 0, "lsem": []}

            def conversions_in():
                for k in range(16):
                    par = k % 2
                    if k >= 2:
                        c.wait("gpsimd", "cvt", k - 1)
                    src = (wgT_hbm, wuT_hbm)[k // 8]
                    ch = k % 8
                    c.dma("gpsimd", stg_i32[par],
                          src[ch * 128:(ch + 1) * 128, :], f"cvc{par}")
                    conv_state["loads"].append((f"cvc{par}", c.cnt[f"cvc{par}"]))

            def conv_cvt_step(n):
                for _ in range(n):
                    k = conv_state["done"]
                    if k >= N_CONV:
                        return
                    if 2 <= k < 16 and conv_state["out"] < k - 1:
                        return
                    conv_state["done"] += 1
                    par = k % 2
                    sem, v = conv_state["loads"][k]
                    c.wait("vector", sem, v)
                    if k < 2:
                        c.wait("vector", "xnw", NTB)
                    if k >= 2:
                        c.wait("vector", f"cvo{par}", 16 * ((k - par) // 2))
                    c.do("vector", "tensor_copy", cw_bf[par], stg_i32[par],
                         inc="cvt")
                    if conv_state["done"] == N_CONV:
                        c.put("conv_all")

            stg_wd = [srlb[:, 0:2048].bitcast(I32), gbuf[:, 0:2048].bitcast(I32)]

            def conv_wd_step(n):
                for _ in range(n):
                    ht = wd_state["done"]
                    if ht >= NHT:
                        return
                    wd_state["done"] += 1
                    while wd_state["loaded"] <= min(ht + 1, NHT - 1):
                        lt = wd_state["loaded"]
                        wd_state["loaded"] += 1
                        sl = lt % 2
                        if lt >= 2:
                            c.wait("gpsimd", "cvtw", lt - 1)
                        c.dma("gpsimd", stg_wd[sl],
                              wdT_hbm[lt * 128:(lt + 1) * 128, :], f"cwc{sl}")
                        wd_state["lsem"].append((f"cwc{sl}", c.cnt[f"cwc{sl}"]))
                    sem, v = wd_state["lsem"][ht]
                    c.wait("gpsimd", sem, v)
                    c.do("gpsimd", "tensor_copy",
                         wdT_sb[:, ht * 1024:(ht + 1) * 1024],
                         stg_wd[ht % 2], inc="cvtw")

            def conv_out_step():
                while conv_state["out"] < min(conv_state["done"], 16):
                    k = conv_state["out"]
                    conv_state["out"] += 1
                    par = k % 2
                    c.wait("scalar", "cvt", k + 1)
                    dst = (wg_scr, wu_scr)[k // 8]
                    ch = k % 8
                    c.dma("scalar", dst[ch * 128:(ch + 1) * 128, :], cw_bf[par],
                          f"cvo{par}")
                    if conv_state["out"] == 16:
                        c.put("conv_done")

            def conv_wait(eng):
                c.wait(eng, "cvo0", c.at("conv_done", "cvo0"))
                c.wait(eng, "cvo1", c.at("conv_done", "cvo1"))

            def ln_phase(b, as_steps=False):
                steps = []

                def step(fn):
                    if as_steps:
                        steps.append(fn)
                    else:
                        fn()

                def x_eng(tb):
                    return "scalar" if (b == 0 and tb % 2 == 1) else "sync"

                st = xs0 if b == 0 else xs
                slot_ap = (xslot0 if b == 0
                           else (lambda s: xbuf[:, s * 1024:(s + 1) * 1024]))

                def p1_tile(tb):
                    par = st.load(x_eng(tb), ("p1", b, tb), slot_ap,
                                  x_hbm[b, tb * 128:(tb + 1) * 128, :])
                    buf = slot_ap(par)
                    st.wait_item("vector", ("p1", b, tb))
                    c.wait("vector", "bn", c.cnt["bn"])
                    c.do("vector", "bn_stats", bnst[:, 0:6],
                         buf[:, 0:512], inc="bn")
                    c.wait("vector", "bn", c.cnt["bn"])
                    c.do("vector", "bn_stats", bnst[:, 6:12],
                         buf[:, 512:1024], inc="bn")
                    c.wait("vector", "bn", c.cnt["bn"])
                    c.do("vector", "bn_aggr", mvAll[:, 2 * tb: 2 * tb + 2],
                         bnst[:, 0:12], inc="bn")
                    st.consumed(par, "bn", c.cnt["bn"])

                nmr_holder = [None]

                def rstd_block():
                    c.wait("vector", "bn", c.cnt["bn"])
                    c.do("vector", "tensor_scalar_add", veps[:, 0:16], mv_var,
                         EPS, inc="bn")
                    c.wait("scalar", "bn", c.cnt["bn"])
                    c.do("scalar", "activation", lnt[:, 0:16], veps[:, 0:16],
                         AF.Ln, inc="rstd")
                    c.wait("scalar", "rstd", c.cnt["rstd"])
                    c.do("scalar", "activation", rstd16[:, 0:16], lnt[:, 0:16],
                         AF.Exp, scale=-0.5, inc="rstd")
                    c.wait("vector", "rstd", c.cnt["rstd"])
                    c.do("vector", "scalar_tensor_tensor", nmr16[:, 0:16],
                         mv_mean, -1.0, rstd16[:, 0:16], ALU.mult, ALU.mult,
                         inc="bn")
                    nmr_holder[0] = c.cnt["bn"]

                def p2_tile(tb):
                    eng = "gpsimd"
                    par = st.load(x_eng(tb), ("p2", b, tb), slot_ap,
                                  x_hbm[b, tb * 128:(tb + 1) * 128, :])
                    st.wait_item(eng, ("p2", b, tb))
                    c.wait(eng, "bn", nmr_holder[0])
                    if b >= 1 and tb == 0:
                        c.wait(eng, "O", c.at(("attn", b - 1), "O"))
                    c.do(eng, "tensor_scalar",
                         xn_sb[:, tb * 1024:(tb + 1) * 1024],
                         slot_ap(par),
                         rstd16[:, tb:tb + 1], nmr16[:, tb:tb + 1],
                         ALU.mult, ALU.add, inc="xnw")
                    st.consumed(par, "xnw", c.cnt["xnw"])
                    if b == 0:
                        xnT_tile(b, tb)
                        if tb % 4 == 3:
                            S0_chunk(b, tb // 4)
                    if tb == NTB - 1:
                        c.put(("xnw", b))

                def rstd_half(lo, hi):
                    c.wait("vector", "bn", c.cnt["bn"])
                    c.do("vector", "tensor_scalar_add", veps[:, lo:hi],
                         mv_var[:, lo:hi], EPS, inc="bn")
                    c.wait("scalar", "bn", c.cnt["bn"])
                    c.do("scalar", "activation", lnt[:, lo:hi], veps[:, lo:hi],
                         AF.Ln, inc="rstd")
                    c.wait("scalar", "rstd", c.cnt["rstd"])
                    c.do("scalar", "activation", rstd16[:, lo:hi], lnt[:, lo:hi],
                         AF.Exp, scale=-0.5, inc="rstd")
                    c.wait("vector", "rstd", c.cnt["rstd"])
                    c.do("vector", "scalar_tensor_tensor", nmr16[:, lo:hi],
                         mv_mean[:, lo:hi], -1.0, rstd16[:, lo:hi],
                         ALU.mult, ALU.mult, inc="bn")
                    nmr_holder[0] = c.cnt["bn"]

                if b == 0 and not as_steps:
                    for tb in range(2):
                        p1_tile(tb)
                    rstd_half(0, 2)
                    for pr in range(1, 8):
                        for tb in range(2):
                            p1_tile(2 * pr + tb)
                            p2_tile(2 * (pr - 1) + tb)
                        rstd_half(2 * pr, 2 * pr + 2)
                    for tb in range(14, NTB):
                        p2_tile(tb)
                    softmax_q(0, 0, embed_tr=False)
                    return None
                for tb in range(NTB):
                    step(lambda tb=tb: p1_tile(tb))
                step(rstd_block)
                for tb in range(NTB):
                    step(lambda tb=tb: p2_tile(tb))
                if as_steps:
                    return steps

            def xnT_tile(b, tb):
                if tb == 0:
                    c.put(("xnTbase", b))
                grp = Tstate["g"]
                bank = (grp % 2) * 1024
                c.wait("tensor", "Tev", grp - 1)
                c.wait("tensor", "xnw", b * NTB + tb + 1)
                if grp == 0:
                    c.wait("tensor", "idld", 16)
                for dt in range(NDT):
                    c.do("tensor", "transpose",
                         psT[:, bank + dt * 128: bank + (dt + 1) * 128],
                         xn_sb[:, tb * 1024 + dt * 128: tb * 1024 + (dt + 1) * 128],
                         identt[:], inc="Ttr" if dt == NDT - 1 else None)
                c.wait("vector", "Ttr", c.cnt["Ttr"])
                if b >= 1 and tb == 0:
                    c.wait("vector", "G", c.cnt["G"])
                    c.wait("vector", "U", c.cnt["U"])
                o = xnT_sb[:].bitcast(I32).rearrange(
                    "p (dt t) -> p dt t", dt=NDT)[:, :, tb * 64:(tb + 1) * 64]
                i_ = psT[:, bank: bank + NDT * 128].bitcast(I32).rearrange(
                    "p (dt t) -> p dt t", dt=NDT)
                c.do("vector", "tensor_copy", o, i_, inc="Tev")
                Tstate["g"] += 1
                if tb == NTB - 1:
                    c.put(("xnT", b))

            def S0_chunk(b, kc):
                if kc == 0:
                    c.wait("tensor", "E", c.cnt["E"])
                    if b > 0:
                        c.wait("tensor", "sG", c.cnt["sG"])
                        c.wait("tensor", "hact", c.cnt["hact"])
                base = c.at(("xnTbase", b), "Tev")
                c.wait("tensor", "Tev", base + 4 * (kc + 1))
                for dt in range(NDT):
                    c.do("tensor", "matmul",
                         psS[:, kc * 512:(kc + 1) * 512],
                         xnT_sb[:, dt * 2048: dt * 2048 + 128],
                         xnT_sb[:, dt * 2048 + kc * 512: dt * 2048 + (kc + 1) * 512],
                         start=(dt == 0), stop=(dt == NDT - 1),
                         inc="S" if (dt == NDT - 1 and kc == 3) else None)

            def tr_group(b, qq, g2):
                grp = Tstate["g"]
                bank = (grp % 2) * 1024
                e0 = (qq % 2) * 2048
                c.wait("tensor", "Tev", grp - 1)
                if g2 == 0:
                    c.wait("tensor", "E", c.at(("E", b, qq), "E"))
                for i in range(8):
                    kt = g2 * 8 + i
                    c.do("tensor", "transpose",
                         psT[:, bank + i * 128: bank + (i + 1) * 128],
                         Ebuf[:, e0 + kt * 128: e0 + (kt + 1) * 128],
                         identt[:], inc="Ttr" if i == 7 else None)
                if g2 == 1:
                    c.put(("Etr", b, qq))
                    Ehist.append(("Etr", b, qq))
                return bank

            def tr_evac(b, qq, g2, bank):
                c.wait("vector", "Ttr", c.cnt["Ttr"])
                c.do("vector", "tensor_copy",
                     aT[:, g2 * 1024:(g2 + 1) * 1024].bitcast(I32),
                     psT[:, bank: bank + 1024].bitcast(I32), inc="Tev")
                c.put(("aTev", b, qq, g2))
                Tstate["g"] += 1

            def softmax_q(b, q, embed_tr):
                if q == 0 and b > 0:
                    c.do("scalar", "activation", lnt[:, 0:1], identt[:, 0:1],
                         AF.Exp)
                c.wait("vector", "S", c.cnt["S"])
                if q >= 2:
                    c.wait("vector", "sevc", c.cnt["sevc"])
                c.do("vector", "tensor_reduce", nmx[:, q % 2: q % 2 + 1],
                     psS[:, 0:2048], AX.X, ALU.max, negate=True, inc="mx")
                if embed_tr:
                    bankB = tr_group(b, q - 1, 1)
                    tr_evac(b, q - 1, 1, bankB)
                c.wait("scalar", "mx", c.cnt["mx"])
                prev_same = [k for k in Ehist if k[2] % 2 == q % 2]
                if prev_same:
                    c.wait("scalar", "Ttr", c.at(prev_same[-1], "Ttr"))
                if q >= 2:
                    c.wait("scalar", "rec", c.cnt["rec"] - 1)
                c.do("scalar", "activation",
                     Ebuf[:, (q % 2) * 2048:(q % 2 + 1) * 2048],
                     psS[:, 0:2048], AF.Exp,
                     bias=nmx[:, q % 2: q % 2 + 1],
                     accum_out=sume[:, q % 2: q % 2 + 1], inc="E")
                c.wait("scalar", "E", c.cnt["E"])
                c.do("scalar", "copy", bnst[:, q % 2: q % 2 + 1],
                     sume[:, q % 2: q % 2 + 1], inc="E")
                c.wait("vector", "E", c.cnt["E"])
                c.do("vector", "reciprocal", recq[:, q % 2: q % 2 + 1],
                     bnst[:, q % 2: q % 2 + 1], inc="rec")
                c.put(("E", b, q))
                c.put(("rec", b, q))

            def attn_phase(b, leftover=None):
                lo = leftover if leftover is not None else []
                assert ("xnT", b) in c.stash

                for q in range(NTB + 1):
                    qq = q - 1
                    conv_out_step()
                    conv_wd_step(2)
                    if q == 0 or q == NTB:
                        conv_cvt_step(3)
                    if q < NTB:
                        if q >= 1:
                            c.wait("tensor", "E", c.cnt["E"])
                            for dt in range(NDT):
                                for kc in range(4):
                                    c.do("tensor", "matmul",
                                         psS[:, kc * 512:(kc + 1) * 512],
                                         xnT_sb[:, dt * 2048 + q * 128: dt * 2048 + (q + 1) * 128],
                                         xnT_sb[:, dt * 2048 + kc * 512: dt * 2048 + (kc + 1) * 512],
                                         start=(dt == 0), stop=(dt == NDT - 1),
                                         inc="S" if (dt == NDT - 1 and kc == 3) else None)
                    if q >= 1:
                        bankA = tr_group(b, qq, 0)
                        tr_evac(b, qq, 0, bankA)
                    if q < NTB and q >= 1:
                        softmax_q(b, q, embed_tr=True)
                        conv_cvt_step(3)
                        for _ in range(2):
                            if lo:
                                lo.pop(0)()
                    if q >= 1:
                        if q == NTB:
                            bankB = tr_group(b, qq, 1)
                            tr_evac(b, qq, 1, bankB)
                        c.wait("tensor", "sevc", c.cnt["sevc"])
                        if b > 0 and qq == 0:
                            c.wait("tensor", "gev", c.at(("ffn", b - 1), "gev"))
                        for g2 in range(2):
                            c.wait("tensor", "Tev", c.at(("aTev", b, qq, g2), "Tev"))
                            for i in range(8):
                                kt = g2 * 8 + i
                                for dh in range(2):
                                    c.do("tensor", "matmul",
                                         psO[:, dh * 512:(dh + 1) * 512],
                                         aT[:, kt * 128:(kt + 1) * 128],
                                         xn_sb[:, kt * 1024 + dh * 512:
                                               kt * 1024 + (dh + 1) * 512],
                                         start=(kt == 0), stop=(kt == 15),
                                         inc="O" if (kt == 15 and dh == 1) else None)
                        c.wait("scalar", "O", c.cnt["O"])
                        c.wait("scalar", "rec", c.at(("rec", b, qq), "rec"))
                        spq = ("spl", b, qq - 2)
                        if spq in spl.items:
                            sem, v = spl.items[spq]
                            c.wait("scalar", sem, v)
                        elif qq < 2 and b > 0:
                            sem, v = spl.items[("spl", b - 1, qq - 2 + NTB)]
                            c.wait("scalar", sem, v)
                        c.do("scalar", "activation",
                             sqb[:, (qq % 2) * 1024:(qq % 2 + 1) * 1024],
                             psO[:, 0:1024], AF.Copy,
                             scale=recq[:, qq % 2: qq % 2 + 1], inc="sevc")
                        c.wait("scalar", "sevc", c.cnt["sevc"])
                        spl.load("scalar", ("spl", b, qq),
                                 lambda s: s_scr[b, qq * 128:(qq + 1) * 128, :],
                                 sqb[:, (qq % 2) * 1024:(qq % 2 + 1) * 1024])
                        if qq == 12:
                            load_chunk(b, 0)
                            load_chunk(b, 1)
                while conv_state["done"] < N_CONV or conv_state["out"] < 16:
                    conv_cvt_step(1)
                    conv_out_step()
                conv_wd_step(NHT)
                c.put(("attn", b))

            wg3 = wg_scr.rearrange("(dt p) hh -> p dt hh", p=128)
            wu3 = wu_scr.rearrange("(dt p) hh -> p dt hh", p=128)

            def load_chunk(b, lidx):
                if lidx >= NTSB * 12:
                    return
                hp = lidx % 12
                if b == 0 and lidx == 0:
                    conv_wait("scalar")
                    conv_wait("sync")
                wgs.load("scalar", ("wg", b, lidx),
                         lambda s: wch[:, s * 2048:(s + 1) * 2048].rearrange(
                             "p (dt h) -> p dt h", h=256),
                         wg3[:, :, hp * 256:(hp + 1) * 256])
                wus.load("sync", ("wu", b, lidx),
                         lambda s: wch[:, (2 + s) * 2048:(3 + s) * 2048].rearrange(
                             "p (dt h) -> p dt h", h=256),
                         wu3[:, :, hp * 256:(hp + 1) * 256])

            def ffn_phase(b, bg_steps=None, epi_tiles=None):
                bg = list(bg_steps) if bg_steps else []
                epi = list(epi_tiles) if epi_tiles else []
                ybank_guard = {"psO": 0, "psU": 0}
                finals = []

                def queue_final(j, tbp, tb):
                    tbg = j * 4 + tbp * 2 + tb
                    gi = b * NTB + tbg
                    par = gi % 2
                    gev_need = c.at(("gevp", b, j, tbp), "gev")

                    def do_loads():
                        spl.wait_item("sync", ("spl", b, tbg))
                        sslot = srs.load("sync", ("srl", gi),
                                         lambda s: srlb[:, s * 1024:(s + 1) * 1024],
                                         s_scr[b, tbg * 128:(tbg + 1) * 128, :])
                        srs.consumed(sslot, "sgm", gi + 1)
                        xslot = xfs.load("sync", ("xfl", gi),
                                         lambda s: xfbuf[:, s * 1024:(s + 1) * 1024],
                                         x_hbm[b, tbg * 128:(tbg + 1) * 128, :])
                        xfs.consumed(xslot, "oadd", 2 * (gi + 1))
                        finals.insert(0, do_mul)

                    def do_mul():
                        meng = "vector" if gi >= 2 * NTB - 4 else "gpsimd"
                        c.wait(meng, "gev", gev_need)
                        srs.wait_item(meng, ("srl", gi))
                        if gi >= 1:
                            c.wait(meng, "oadd", 2 * gi)
                        c.do(meng, "tensor_tensor",
                             sgbuf[:, 0:1024],
                             srlb[:, par * 1024:(par + 1) * 1024],
                             gbuf[:, (gi % 4) * 1024:(gi % 4 + 1) * 1024],
                             ALU.mult, inc="sgm")
                        sgm_need = c.cnt["sgm"]

                        def do_adds():
                            oadds = []
                            tail = gi >= 2 * NTB - 4
                            obuf = (xnT_sb[:].bitcast(F32)
                                    [:, (gi % 4) * 1024:(gi % 4 + 1) * 1024]
                                    if tail else outbuf[:, 0:1024])
                            for half in range(2):
                                c.wait("vector", "sgm", sgm_need)
                                xfs.wait_item("vector", ("xfl", gi))
                                osem = f"ost{half}"
                                if not tail:
                                    c.wait("vector", osem, c.cnt[osem])
                                c.do("vector", "tensor_tensor",
                                     obuf[:, half * 512:(half + 1) * 512],
                                     sgbuf[:, half * 512:(half + 1) * 512],
                                     xfbuf[:, par * 1024 + half * 512:
                                           par * 1024 + (half + 1) * 512],
                                     ALU.add, inc="oadd")
                                oadds.append(c.cnt["oadd"])

                            def do_stores():
                                for half in range(2):
                                    osem = f"ost{half}"
                                    c.wait("scalar", "oadd", oadds[half])
                                    c.dma("scalar",
                                          out_hbm[b, tbg * 128:(tbg + 1) * 128,
                                                  half * 512:(half + 1) * 512],
                                          obuf[:, half * 512:(half + 1) * 512],
                                          osem)
                            finals.insert(0, do_stores)
                        finals.insert(0, do_adds)
                    finals.append(do_loads)

                def drain_final(n=1):
                    for _ in range(n):
                        if finals:
                            finals.pop(0)()

                for j in range(NTSB):
                    for hp in range(12):
                        gidx = j * 12 + hp
                        buf = gidx % 2
                        for hh in range(2):
                            ht = hp * 2 + hh
                            c.wait("tensor", "sG", c.cnt["sG"] - 1)
                            c.wait("tensor", "hact", c.cnt["hact"] - 1)
                            if j == 0 and ht == 0:
                                c.wait("tensor", "E", c.at(("attn", b), "E"))
                            wgs.wait_item("tensor", ("wg", b, gidx))
                            wus.wait_item("tensor", ("wu", b, gidx))
                            hb = (ht % 2) * 512
                            for dt in range(NDT):
                                c.do("tensor", "matmul", psG[:, hb: hb + 512],
                                     wch[:, buf * 2048 + dt * 256 + hh * 128:
                                         buf * 2048 + dt * 256 + (hh + 1) * 128],
                                     xnT_sb[:, dt * 2048 + j * 512: dt * 2048 + (j + 1) * 512],
                                     start=(dt == 0), stop=(dt == NDT - 1),
                                     inc="G" if dt == NDT - 1 else None)
                            if ht == 0:
                                c.wait("tensor", "gev", ybank_guard["psU"])
                            for dt in range(NDT):
                                c.do("tensor", "matmul", psU[:, hb: hb + 512],
                                     wch[:, (2 + buf) * 2048 + dt * 256 + hh * 128:
                                         (2 + buf) * 2048 + dt * 256 + (hh + 1) * 128],
                                     xnT_sb[:, dt * 2048 + j * 512: dt * 2048 + (j + 1) * 512],
                                     start=(dt == 0), stop=(dt == NDT - 1),
                                     inc="U" if dt == NDT - 1 else None)
                            if j == 0 and hp == 0 and hh == 0:
                                c.do("scalar", "activation", lnt[:, 0:1],
                                     identt[:, 0:1], AF.Silu)
                            c.wait("scalar", "G", c.cnt["G"])
                            c.wait("scalar", "hact", c.cnt["hact"])
                            c.do("scalar", "activation",
                                 sGb[:, 0:512],
                                 psG[:, hb: hb + 512], AF.Silu, scale=SCALE_G, inc="sG")
                            c.wait("vector", "U", c.cnt["U"])
                            c.wait("vector", "sG", c.cnt["sG"])
                            if b > 0 or j > 0:
                                key = ("Ytot", b, j - 1) if j > 0 else ("Ytot", b - 1, NTSB - 1)
                                c.wait("vector", "Y", c.at(key, "Y"))
                            if b == 0 and j == 0 and ht == 0:
                                c.wait("vector", "cvt", c.at("conv_all", "cvt"))
                                c.wait("vector", "cvtw", NHT)
                            c.do("vector", "scalar_tensor_tensor",
                                 hact[:, ht * 512:(ht + 1) * 512],
                                 psU[:, hb: hb + 512], 1.0,
                                 sGb[:, 0:512],
                                 ALU.mult, ALU.mult, inc="hact")
                            c.put(("hactd", b, j, ht))
                        c.put(("wchG", b, gidx))
                        wgs.consumed(buf, "G", c.at(("wchG", b, gidx), "G"))
                        wus.consumed(buf, "U", c.at(("wchG", b, gidx), "U"))
                        load_chunk(b, gidx + 2)
                        drain_final(2)
                    for tbp in range(2):
                        for dh in range(2):
                            borrow = (dh == 1 and (j != NTSB - 1 or not epi))
                            ybank = psU if borrow else psY
                            guard_key = "psU" if borrow else "psO"
                            c.wait("tensor", "gev", ybank_guard[guard_key])
                            if j == 0 and tbp == 0 and dh == 0:
                                c.wait("tensor", "sevc", c.at(("attn", b), "sevc"))
                                if b == 0:
                                    c.wait("tensor", "cvtw", NHT)
                            if borrow and tbp == 0:
                                c.wait("tensor", "hact",
                                       c.at(("hactd", b, j, NHT - 1), "hact"))
                            for ht in range(NHT):
                                if tbp == 0 and dh == 0:
                                    c.wait("tensor", "hact",
                                           c.at(("hactd", b, j, ht), "hact"))
                                for tb in range(2):
                                    c.do("tensor", "matmul",
                                         ybank[:, tb * 512:(tb + 1) * 512],
                                         hact[:, ht * 512 + (tbp * 2 + tb) * 128:
                                              ht * 512 + (tbp * 2 + tb + 1) * 128],
                                         wdT_sb[:, ht * 1024 + dh * 512:
                                                ht * 1024 + (dh + 1) * 512],
                                         start=(ht == 0), stop=(ht == NHT - 1),
                                         inc="Y" if (ht == NHT - 1 and tb == 1) else None)
                            if j == NTSB - 1 and epi:
                                for _ in range(4):
                                    if epi:
                                        epi.pop(0)()
                            c.wait("scalar", "Y", c.cnt["Y"])
                            for tb in range(2):
                                tbg = j * 4 + tbp * 2 + tb
                                gi = b * NTB + tbg
                                if dh == 0 and gi >= 4:
                                    c.wait("scalar", "sgm", gi - 3)
                                c.do("scalar", "activation",
                                     gbuf[:, (gi % 4) * 1024 + dh * 512:
                                          (gi % 4) * 1024 + (dh + 1) * 512],
                                     ybank[:, tb * 512:(tb + 1) * 512], AF.Silu,
                                     scale=SCALE_Y, inc="gev")
                            ybank_guard[guard_key] = c.cnt["gev"]
                            if j == NTSB - 1 and tbp == 1 and dh == 1:
                                while epi:
                                    epi.pop(0)()
                            if b == NB - 1 and j == NTSB - 1:
                                drain_final(3)
                            for _ in range(3):
                                if bg:
                                    bg.pop(0)()
                        c.put(("gevp", b, j, tbp))
                        for tb in range(2):
                            queue_final(j, tbp, tb)
                    c.put(("Ytot", b, j))
                c.put(("ffn", b))
                return finals

            ln_phase(0)
            conversions_in()
            attn_phase(0)
            lo1 = ffn_phase(0, bg_steps=ln_phase(1, as_steps=True),
                            epi_tiles=[
                                (lambda tb=tb: (
                                    xnT_tile(1, tb),
                                    S0_chunk(1, tb // 4) if tb % 4 == 3 else None))
                                for tb in range(NTB)]
                            + [lambda: softmax_q(1, 0, embed_tr=False)])
            attn_phase(1, leftover=lo1)
            rest = ffn_phase(1)
            while rest:
                rest.pop(0)()

            if "s" in dbg:
                for b in range(NB):
                    for qq in range(NTB):
                        spl.wait_item("gpsimd", ("spl", b, qq))
                c.dma("gpsimd", dbg["s"][:, :, :], s_scr[:, :, :], "dbgs")
            if "xn" in dbg:
                for tb in range(NTB):
                    c.dma("gpsimd", dbg["xn"][NB - 1, tb * 128:(tb + 1) * 128, :],
                          xn_sb[:, tb * 1024:(tb + 1) * 1024], "dbgs")
            if "xnT" in dbg:
                for dt in range(NDT):
                    c.dma("gpsimd", dbg["xnT"][NB - 1, dt * 128:(dt + 1) * 128, :],
                          xnT_sb[:, dt * 2048:(dt + 1) * 2048], "dbgs")

            c.wait("scalar", "ost0", c.cnt["ost0"])
            c.wait("scalar", "ost1", c.cnt["ost1"])
            if dbg:
                c.wait("gpsimd", "dbgs", c.cnt["dbgs"])

        base = {"tensor": None, "vector": None, "scalar": None,
                "sync": None, "gpsimd": None}

        @block.tensor
        def _(tensor):
            walk("tensor", {**base, "tensor": tensor})

        @block.vector
        def _(vector):
            walk("vector", {**base, "vector": vector})

        @block.scalar
        def _(scalar):
            walk("scalar", {**base, "scalar": scalar})

        @block.sync
        def _(sync):
            walk("sync", {**base, "sync": sync})

        @block.gpsimd
        def _(gpsimd):
            walk("gpsimd", {**base, "gpsimd": gpsimd})

    return nc


def kernel(L0_attn_x, L0_ffn_w_gate, L0_ffn_w_up, L0_ffn_w_down):
    x = np.ascontiguousarray(np.asarray(L0_attn_x, dtype=np.float32))
    wgT = np.ascontiguousarray(np.asarray(L0_ffn_w_gate, dtype=np.int32).T)
    wuT = np.ascontiguousarray(np.asarray(L0_ffn_w_up, dtype=np.int32).T)
    wdT = np.ascontiguousarray(np.asarray(L0_ffn_w_down, dtype=np.int32).T)
    ident = np.eye(128, dtype=ml_dtypes.bfloat16)

    nc = build_program()
    in_maps = []
    for core in range(8):
        in_maps.append({
            "x": x[core * NB:(core + 1) * NB],
            "wgT": wgT, "wuT": wuT, "wdT": wdT,
            "ident": ident,
        })
    res = run_bass_kernel_spmd(nc, in_maps, core_ids=list(range(8)))
    out = np.concatenate([res.results[i]["out"] for i in range(8)], axis=0)
    if DEBUG_OUTPUTS:
        kernel.debug = [{k: np.asarray(res.results[i]["dbg_" + k])
                         for k in DEBUG_OUTPUTS} for i in range(8)]
    return out
